# version 21
# baseline (speedup 1.0000x reference)
import numpy as np
import concourse.bass as bass
import concourse.mybir as mybir
from concourse.bass_utils import run_bass_kernel_spmd

F32 = mybir.dt.float32
AX = mybir.AxisListType
ALU = mybir.AluOpType
ACT = mybir.ActivationFunctionType

NCORE = 8
DBG = False
PARANOID = False
PARANOID_ENGS = ("vector",)
BPC = 2
L = 2048
D = 1024
T = 128
NTILE = L // T
NCHK = D // 128
H = 4
DQH = 128
DVH = 256
CH = 64
NCH = T // CH
LN_EPS = 1e-5
NORM_EPS = 1e-5
ENGS = ("sync", "tensor", "vector", "scalar", "gpsimd")
PROF_ANNOTATE = False


class Op:
    __slots__ = ("eng", "fn", "deps", "dma", "needed", "pos", "cum", "tag")

    def __init__(self, eng, fn, dma, tag):
        self.eng = eng
        self.fn = fn
        self.dma = dma
        self.needed = False
        self.deps = []
        self.pos = -1
        self.cum = None
        self.tag = tag


class Builder:
    def __init__(self):
        self.plan = {e: [] for e in ENGS}
        self.res = {}

    def op(self, eng, fn, reads=(), writes=(), tag="", dma=False, force=False):
        o = Op(eng, fn, dma, tag)
        if force or dma:
            o.needed = True
        o.pos = len(self.plan[eng])
        deps = {}

        def dep(p):
            if p is None:
                return
            if (p.eng == eng and eng != "gpsimd" and not p.dma
                    and not PARANOID and eng not in PARANOID_ENGS):
                return
            cur = deps.get(p.eng)
            if cur is None or p.pos > cur.pos:
                deps[p.eng] = p

        for r in reads:
            ent = self.res.get(r)
            if ent is not None:
                dep(ent[0])
        for w in writes:
            ent = self.res.get(w)
            if ent is not None:
                dep(ent[0])
                for rd in ent[1]:
                    dep(rd)
        o.deps = list(deps.values())
        for p in o.deps:
            p.needed = True
        for r in reads:
            self.res.setdefault(r, [None, []])[1].append(o)
        for w in writes:
            self.res[w] = [o, []]
        self.plan[eng].append(o)
        return o

    def finalize(self):
        for e in ENGS:
            cum_c = 0
            cum_d = 0
            for o in self.plan[e]:
                if o.needed:
                    if o.dma:
                        cum_d += 16
                        o.cum = cum_d
                    else:
                        cum_c += 1
                        o.cum = cum_c

    def replay(self, eng_name, eng, sems, dma_window=1):
        csem, dsem = sems[eng_name]
        ndma = 0
        for o in self.plan[eng_name]:
            if o.dma:
                ndma += 1
                if ndma > dma_window:
                    eng.wait_ge(dsem, 16 * (ndma - dma_window))
            waitv = {}
            for p in o.deps:
                if p.eng == eng_name and p.pos >= o.pos:
                    raise RuntimeError("self-dep ordering bug")
                key = (p.eng, p.dma)
                v = waitv.get(key)
                if v is None or p.cum > v:
                    waitv[key] = p.cum
            for (pe, pdma), v in waitv.items():
                s = sems[pe][1 if pdma else 0]
                eng.wait_ge(s, v)
            inst = o.fn(eng)
            if PROF_ANNOTATE and o.tag:
                inst.annotate(o.tag)
            if o.needed:
                inst.then_inc(dsem if o.dma else csem, 16 if o.dma else 1)


def build_program():
    nc = bass.Bass("TRN2", target_bir_lowering=False, detect_race_conditions=False)
    x_h = nc.dram_tensor("x_in", [BPC, D, L], F32, kind="ExternalInput").ap()
    y_h = nc.dram_tensor("y_out", [BPC, D, L], F32, kind="ExternalOutput").ap()
    wq_h = nc.dram_tensor("wq", [D, 512], F32, kind="ExternalInput").ap()
    wk_h = nc.dram_tensor("wk", [D, 512], F32, kind="ExternalInput").ap()
    wv_h = nc.dram_tensor("wv", [D, 1024], F32, kind="ExternalInput").ap()
    wg_h = nc.dram_tensor("wg", [D, 1024], F32, kind="ExternalInput").ap()
    wo_h = nc.dram_tensor("wo", [1024, D], F32, kind="ExternalInput").ap()
    wb_h = nc.dram_tensor("wb", [D, H], F32, kind="ExternalInput").ap()
    cw_h = nc.dram_tensor("cw", [D, 4], F32, kind="ExternalInput").ap()
    acr_h = nc.dram_tensor("acr", [128, 2, 4, 4, 32], F32, kind="ExternalInput").ap()
    bcr_h = nc.dram_tensor("bcr", [128, 2, 4, 16], F32, kind="ExternalInput").ap()
    bias_h = nc.dram_tensor("biasri", [128, 8], F32, kind="ExternalInput").ap()
    nw_h = nc.dram_tensor("normw", [1, 256], F32, kind="ExternalInput").ap()

    dbg_tensors = {}

    def declare_dbg(name, shape):
        dbg_tensors[name] = nc.dram_tensor("dbg_" + name, list(shape), F32,
                                           kind="ExternalOutput").ap()
        return dbg_tensors[name]

    import contextlib
    ctx = contextlib.ExitStack()
    with ctx:
        sb = lambda name, shape: ctx.enter_context(nc.sbuf_tensor(name, shape, F32))
        wq = sb("wq_s", [128, NCHK, 512])
        wk = sb("wk_s", [128, NCHK, 512])
        wv = sb("wv_s", [128, NCHK, 1024])
        wg = sb("wg_s", [128, NCHK, 1024])
        wo = sb("wo_s", [128, NCHK, 1024])
        wb = sb("wb_s", [128, NCHK, H])
        cw = sb("cw_s", [128, NCHK, 4])
        wtr = sb("wtr_s", [128, 4, 512])
        wti = sb("wti_s", [128, 4, 512])
        bias = sb("bias_s", [128, 8])
        nw1 = sb("nw1_s", [1, 256])
        nwb = sb("nwb_s", [128, 256])
        ones_c = sb("ones_c", [128, 1])
        ones_r = sb("ones_r", [1, 128])
        ident = sb("ident_s", [128, 128])
        msl = sb("msl_s", [128, 128])
        mui = sb("mui_s", [128, 64])
        x2_ = [sb("xa_s", [128, NCHK, T]), sb("xb_s", [128, NCHK, T])]
        xln = sb("xln_s", [128, NCHK, T + 3])
        h = sb("h_s", [128, NCHK, T])
        qf = sb("qf_s", [128, H, T])
        kf = sb("kf_s", [128, H, T])
        kt = sb("kt_s", [128, 512])
        vt = sb("vt_s", [128, 1024])
        ot = sb("ot_s", [128, 1024])

        S = sb("S_s", [128, H, DVH])
        tinv = sb("tinv_s", [128, H, 128])
        yz = sb("yz_s", [128, 512])
        xv = sb("xv_s", [128, 256])
        mq = sb("mq_s", [128, 256])
        wp = sb("wp_s", [128, 1024])
        uu = sb("uu_s", [128, 1024])
        cbc2_ = [sb("cbca_s", [128, 256]), sb("cbcb_s", [128, 256])]
        csm = sb("csm_s", [128, 8])
        halfb = sb("halfb_s", [128, 8])
        negone = sb("negone_s", [128, 1])
        bt = sb("bt_s", [128, H])
        nbt = sb("nbt_s", [128, H])
        rq = sb("rq_s", [128, H])
        rk = sb("rk_s", [128, H])
        big = sb("big_s", [128, 1024])
        tmp = sb("tmp_s", [128, 512])
        halo = sb("halo_s", [128, NCHK, 3])
        st1 = sb("st1_s", [1, 256])
        st2 = sb("st2_s", [1, 256])
        st3 = sb("st3_s", [1, 256])
        pb = [ctx.enter_context(nc.psum_tensor(f"pb{i}", [128, 512], F32))
              for i in range(8)]

        sems = {}
        for e in ENGS:
            sems[e] = (ctx.enter_context(nc.semaphore(f"c_{e}")),
                       ctx.enter_context(nc.semaphore(f"d_{e}")))

        b = Builder()

        def dump(name, ap, reads):
            if not DBG:
                return
            d = declare_dbg(name, ap.shape)
            b.op("scalar", lambda E, d=d, a=ap: E.dma_start(out=d, in_=a),
                 reads, (), dma=True, force=True, tag="dbg_" + name)

        def dma(eng, dst, src, reads, writes, tag):
            b.op(eng, lambda E, d=dst, s=src: E.dma_start(out=d, in_=s),
                 reads=reads, writes=writes, dma=True, tag=tag)

        src00 = x_h[0].rearrange("(c p) t -> p c t", p=128)[:, :, 0:T]
        dma("sync", x2_[0][:], src00, (), tuple(f"x0_{c}" for c in range(NCHK)),
            "s0t0_ldx")
        dma("sync", big[:].rearrange("p (r c k b) -> p r c k b", r=2, c=4, k=4),
            acr_h, (), ("big",), "ldacr")
        dma("sync", tmp[:, 0:128].rearrange("p (r c e) -> p r c e", r=2, c=4),
            bcr_h, (), ("tmp0",), "ldbcr")
        dma("sync", bias[:], bias_h, (), ("bias",), "ldbias")
        dma("sync", nw1[:], nw_h, (), ("nw1",), "ldnw")
        dma("sync", cw[:], cw_h.rearrange("(k p) n -> p k n", p=128), (), ("cw",), "ldcw")
        dma("sync", wb[:], wb_h.rearrange("(k p) n -> p k n", p=128), (), ("wb",), "ldwb")
        dma("scalar", wq[:], wq_h.rearrange("(k p) n -> p k n", p=128), (), ("wq",), "ldwq")
        dma("scalar", wk[:], wk_h.rearrange("(k p) n -> p k n", p=128), (), ("wk",), "ldwk")
        dma("scalar", wv[:, :, 0:512],
            wv_h.rearrange("(k p) n -> p k n", p=128)[:, :, 0:512], (), ("wv_lo",), "ldwvl")
        dma("gpsimd", wv[:, :, 512:1024],
            wv_h.rearrange("(k p) n -> p k n", p=128)[:, :, 512:1024], (), ("wv_hi",), "ldwvh")
        dma("gpsimd", wg[:], wg_h.rearrange("(k p) n -> p k n", p=128), (), ("wg",), "ldwg")
        dma("gpsimd", wo[:], wo_h.rearrange("(k p) n -> p k n", p=128), (), ("wo",), "ldwo")

        b.op("gpsimd", lambda E: E.memset(ones_c[:], 1.0), (), ("ones_c",))
        b.op("gpsimd", lambda E: E.memset(ones_r[:], 1.0), (), ("ones_r",))
        b.op("gpsimd", lambda E: E.memset(ident[:], 0.0), (), ("ident",))
        b.op("gpsimd", lambda E: E.affine_select(
            out=ident[:], in_=ident[:], compare_op=ALU.not_equal, fill=1.0,
            base=0, pattern=[[-1, 128]], channel_multiplier=1),
            ("ident",), ("ident",))
        b.op("gpsimd", lambda E: E.memset(msl[:], 1.0), (), ("msl",))
        b.op("gpsimd", lambda E: E.affine_select(
            out=msl[0:64, :], in_=msl[0:64, :], compare_op=ALU.is_ge, fill=0.0,
            base=-1, pattern=[[-1, 128]], channel_multiplier=1),
            ("msl",), ("msl",))
        b.op("gpsimd", lambda E: E.affine_select(
            out=msl[64:128, :], in_=msl[64:128, :], compare_op=ALU.is_ge, fill=0.0,
            base=63, pattern=[[-1, 128]], channel_multiplier=1),
            ("msl",), ("msl",))
        b.op("gpsimd", lambda E: E.affine_select(
            out=msl[64:128, :], in_=msl[64:128, :], compare_op=ALU.is_ge, fill=0.0,
            base=-64, pattern=[[1, 128]], channel_multiplier=0),
            ("msl",), ("msl",))
        b.op("gpsimd", lambda E: E.memset(mui[:], 1.0), (), ("mui",))
        b.op("gpsimd", lambda E: E.affine_select(
            out=mui[0:64, :], in_=mui[0:64, :], compare_op=ALU.is_ge, fill=0.0,
            base=0, pattern=[[1, 64]], channel_multiplier=-1),
            ("mui",), ("mui",))
        b.op("gpsimd", lambda E: E.affine_select(
            out=mui[64:128, :], in_=mui[64:128, :], compare_op=ALU.is_ge, fill=0.0,
            base=0, pattern=[[1, 64]], channel_multiplier=-1),
            ("mui",), ("mui",))

        b.op("tensor", lambda E: E.matmul(pb[7][:, 0:256], ones_r[:], nw1[:],
                                          start=True, stop=True),
             ("ones_r", "nw1"), ("b7",))
        b.op("vector", lambda E: E.tensor_copy(nwb[:], pb[7][:, 0:256]),
             ("b7",), ("nwb",))

        acr_v = big[:].rearrange("p (r c k b) -> p r c k b", r=2, c=4, k=4)
        bcr_v = tmp[:, 0:128].rearrange("p (r c e) -> p r c e", r=2, c=4)
        for ri, (wt, wname) in enumerate(((wtr, "wtr"), (wti, "wti"))):
            for kc in range(4):
                for c in range(4):
                    a_ap = acr_v[:, ri, c, kc, :].rearrange(
                        "p (b o) -> p b o", o=1).broadcast_to([128, 32, 16])
                    b_ap = bcr_v[:, ri, c, :].rearrange(
                        "p (o e) -> p o e", o=1).broadcast_to([128, 32, 16])
                    dst = wt[:, kc, :].rearrange("p (b e) -> p b e", e=16)
                    if c == 0:
                        b.op("vector",
                             lambda E, a=a_ap, bb=b_ap, d=dst: E.tensor_mul(d, a, bb),
                             ("big", "tmp0"), (wname,))
                    else:
                        tview = uu[:, 0:512].rearrange("p (b e) -> p b e", e=16)
                        b.op("vector",
                             lambda E, a=a_ap, bb=b_ap, d=tview: E.tensor_mul(d, a, bb),
                             ("big", "tmp0"), ("uu",))
                        b.op("vector",
                             lambda E, d=dst, t=tview: E.tensor_add(d, d, t),
                             ("uu", wname), (wname,))

        b.op("gpsimd", lambda E: E.memset(negone[:], -1.0), (), ("negone",))
        b.op("vector", lambda E: E.tensor_scalar(msl[:], msl[:], -1.0, None, ALU.mult),
             ("msl",), ("msl",), "mslneg")
        first_cs = [True]
        for m in range(8):
            ri = 0 if m < 4 else 1
            mo = m % 4
            for k in range(4):
                st = first_cs[0]; first_cs[0] = False
                w1 = wtr if ri == 0 else wti
                r1 = "wtr" if ri == 0 else "wti"
                b.op("tensor", lambda E, w1=w1, k=k, mo=mo, m=m, st=st: E.matmul(
                    pb[6][:, m:m + 1], w1[:, k, mo * 128:(mo + 1) * 128],
                    ones_c[:], start=st, stop=False),
                    (r1, "ones_c"), ("b6",), "cs1")
            for k in range(4):
                w2 = wti if ri == 0 else wtr
                r2 = "wti" if ri == 0 else "wtr"
                o2_ = negone if ri == 0 else ones_c
                or2 = "negone" if ri == 0 else "ones_c"
                b.op("tensor", lambda E, w2=w2, k=k, mo=mo, m=m,
                     o2_=o2_, last=(m == 7 and k == 3): E.matmul(
                    pb[6][:, m:m + 1], w2[:, k, mo * 128:(mo + 1) * 128],
                    o2_[:], start=False, stop=last),
                    (r2, or2), ("b6",), "cs2")
        b.op("vector", lambda E: E.tensor_copy(csm[:], pb[6][:, 0:8]),
             ("b6",), ("csm",), "csev")
        b.op("vector", lambda E: E.tensor_scalar(halfb[:], bias[:], 0.5, None, ALU.mult),
             ("bias",), ("halfb",), "hbv")

        def ln_stats_mms(i, bank, tag):
            par = i % 2
            x = x2_[par]
            pk = pb[bank]
            for c in range(NCHK):
                b.op("tensor",
                     lambda E, c=c, x=x, pk=pk, s=(c == 0): E.matmul(
                         pk[0:1, 0:T], ones_c[:], x[:, c, :], start=s, stop=False),
                     ("ones_c", f"x{par}_{c}"), (f"b{bank}",), tag=tag + "sum")
            for c in range(NCHK):
                slot = (c % 2 + 2 * par) * T
                b.op("scalar",
                     lambda E, c=c, x=x, slot=slot: E.activation(
                         tmp[:, slot:slot + T], x[:, c, :], ACT.Square),
                     (f"x{par}_{c}",), (f"tmp{slot // T}",), tag=tag + "sq")
                b.op("tensor",
                     lambda E, pk=pk, slot=slot, last=(c == NCHK - 1): E.matmul(
                         pk[0:1, T:2 * T], ones_c[:], tmp[:, slot:slot + T],
                         start=False, stop=last),
                     ("ones_c", f"tmp{slot // T}"), (f"b{bank}",), tag=tag + "ssq")

        def ln_stats_smalls(i, bank, tag):
            pk = pb[bank]
            b.op("vector", lambda E, pk=pk: E.tensor_scalar(
                st1[:, 0:T], pk[0:1, 0:T], 1.0 / D, None, ALU.mult),
                (f"b{bank}",), ("st1",), tag=tag + "mu")
            b.op("vector", lambda E, pk=pk: E.tensor_scalar(
                st2[:, 0:T], pk[0:1, T:2 * T], 1.0 / D, None, ALU.mult),
                (f"b{bank}",), ("st2",), tag=tag + "ms")
            b.op("vector", lambda E: E.tensor_mul(
                st3[:, 0:T], st1[:, 0:T], st1[:, 0:T]),
                ("st1",), ("st3",), tag=tag + "musq")
            b.op("vector", lambda E: E.tensor_sub(
                st2[:, 0:T], st2[:, 0:T], st3[:, 0:T]),
                ("st2", "st3"), ("st2",), tag=tag + "var")
            b.op("vector", lambda E: E.tensor_scalar(
                st2[:, 0:T], st2[:, 0:T], LN_EPS, None, ALU.add),
                ("st2",), ("st2",), tag=tag + "eps")
            b.op("scalar", lambda E: E.activation(st2[:, 0:T], st2[:, 0:T], ACT.Sqrt),
                 ("st2",), ("st2",), tag=tag + "sqrt")
            b.op("vector", lambda E: E.reciprocal(st2[:, 0:T], st2[:, 0:T]),
                 ("st2",), ("st2",), tag=tag + "rc")

        def ln_stats_fin(i, bank, tag, mode):
            par = i % 2
            pk = pb[bank]
            if mode == "fold":
                b.op("vector", lambda E: E.tensor_scalar(
                    st1[:, 0:T], st1[:, 0:T], -1.0, None, ALU.mult),
                    ("st1",), ("st1",), tag=tag + "c2")
                b.op("vector", lambda E: E.tensor_scalar(
                    st2[:, 0:T], st2[:, 0:T], 0.5, None, ALU.mult),
                    ("st2",), ("st2",), tag=tag + "c1")
            b.op("tensor", lambda E, pk=pk: E.matmul(pk[:, 0:T], ones_r[:], st2[:, 0:T],
                                                     start=True, stop=False),
                 ("ones_r", "st2"), (f"b{bank}",), tag=tag + "bc1")
            b.op("tensor", lambda E, pk=pk: E.matmul(pk[:, T:2 * T], ones_r[:], st1[:, 0:T],
                                                     start=False, stop=True),
                 ("ones_r", "st1"), (f"b{bank}",), tag=tag + "bc2")
            b.op("vector", lambda E, pk=pk, cb=cbc2_[par]: E.tensor_copy(
                cb[:, 0:2 * T], pk[:, 0:2 * T]),
                (f"b{bank}",), (f"cbc{par}",), tag=tag + "cbev")

        def ln_stats(i, bank, tag, mode):
            ln_stats_mms(i, bank, tag)
            ln_stats_smalls(i, bank, tag)
            ln_stats_fin(i, bank, tag, mode)

        def ffn_negs(i, tag):
            par = i % 2
            x = x2_[par]
            for k in range(4, 8):
                b.op("vector", lambda E, k=k, x=x: E.tensor_scalar(
                    uu[:, (k - 4) * T:(k - 3) * T], x[:, k, :], -1.0, None, ALU.mult),
                    (f"x{par}_{k}",), ("uu",), tag=tag + "neg")

        def ffn_wave(i, ms, banks, tag):
            par = i % 2
            x = x2_[par]
            for k in range(8):
                for m, bk in zip(ms, banks):
                    ri = 0 if m < 4 else 1
                    mo = m % 4
                    if ri == 0:
                        wt_, nm_ = (wtr, "wtr") if k < 4 else (wti, "wti")
                    else:
                        wt_, nm_ = (wti, "wti") if k < 4 else (wtr, "wtr")
                    lhs = wt_[:, k % 4, mo * 128:(mo + 1) * 128]
                    if ri == 0 and k >= 4:
                        rhs, rr = uu[:, (k - 4) * T:(k - 3) * T], ("uu",)
                    else:
                        rhs, rr = x[:, k, :], (f"x{par}_{k}",)
                    b.op("tensor",
                         lambda E, l=lhs, r=rhs, bk=bk, s=(k == 0), p=(k == 7):
                         E.matmul(pb[bk][:, 0:T], l, r, start=s, stop=p),
                         (nm_,) + rr, (f"b{bk}",), tag=tag + "mm")

        def ffn_epi(i, ms, banks, tag):
            par = i % 2
            x = x2_[par]
            cb = cbc2_[par]
            for m, bk in zip(ms, banks):
                b.op("vector", lambda E, m=m, bk=bk, cb=cb: E.scalar_tensor_tensor(
                    tmp[:, 0:T], cb[:, T:2 * T], csm[:, m:m + 1], pb[bk][:, 0:T],
                    ALU.mult, ALU.add),
                    (f"cbc{par}", "csm", f"b{bk}"), ("tmp0",), tag=tag + "ep1")
                b.op("vector", lambda E, cb=cb: E.tensor_mul(
                    tmp[:, 0:T], tmp[:, 0:T], cb[:, 0:T]),
                    ("tmp0", f"cbc{par}"), ("tmp0",), tag=tag + "ep2")
                b.op("vector", lambda E, m=m, x=x: E.scalar_tensor_tensor(
                    x[:, m, :], tmp[:, 0:T], halfb[:, m:m + 1], x[:, m, :],
                    ALU.add, ALU.add),
                    ("tmp0", "halfb", f"x{par}_{m}"), (f"x{par}_{m}",), tag=tag + "ep3")

        def tile_front(i, seq, t):
            par = i % 2
            x = x2_[par]
            cb = cbc2_[par]
            tag = f"s{seq}t{t}_"
            ln_stats(i, 7, tag + "ln2_", "apply")
            if t == 0:
                b.op("gpsimd", lambda E: E.memset(halo[:], 0.0),
                     (), ("halo",), tag + "hz")
            b.op("vector", lambda E: E.tensor_copy(xln[:, :, 0:3], halo[:]),
                 ("halo",) + tuple(f"xl{c}" for c in range(NCHK)),
                 ("halo_r",), tag + "hr")
            for c in range(NCHK):
                b.op("vector", lambda E, c=c, x=x, cb=cb: E.tensor_sub(
                    xln[:, c, 3:3 + T], x[:, c, :], cb[:, T:2 * T]),
                    (f"x{par}_{c}", f"cbc{par}", "halo_r"), (f"xl{c}",), tag + "ap1")
                b.op("vector", lambda E, c=c, cb=cb: E.tensor_mul(
                    xln[:, c, 3:3 + T], xln[:, c, 3:3 + T], cb[:, 0:T]),
                    (f"xl{c}", f"cbc{par}"), (f"xl{c}",), tag + "ap2")
                b.op("vector", lambda E, c=c: E.tensor_scalar(
                    h[:, c, :], xln[:, c, 0:T], cw[:, c, 0:1], None, ALU.mult),
                    (f"xl{c}", "halo_r", "cw"), (f"h{c}",), tag + "cv0")
                for ii in range(1, 4):
                    b.op("vector", lambda E, c=c, ii=ii: E.scalar_tensor_tensor(
                        h[:, c, :], xln[:, c, ii:ii + T], cw[:, c, ii:ii + 1],
                        h[:, c, :], ALU.mult, ALU.add),
                        (f"xl{c}", "halo_r", "cw", f"h{c}"), (f"h{c}",), tag + f"cv{ii}")
                b.op("scalar", lambda E, c=c: E.activation(
                    h[:, c, :], h[:, c, :], ACT.Silu),
                    (f"h{c}",), (f"h{c}",), tag + "silu")
                for (bk, w_, wr, n0, n1) in ((0, wq, "wq", 0, 512),
                                             (1, wk, "wk", 0, 512),
                                             (2, wv, "wv_lo", 0, 512),
                                             (3, wv, "wv_hi", 512, 1024),
                                             (4, wb, "wb", 0, H)):
                    b.op("tensor", lambda E, c=c, bk=bk, w_=w_, n0=n0, n1=n1,
                         s=(c == 0), p=(c == NCHK - 1): E.matmul(
                        pb[bk][:, 0:n1 - n0], h[:, c, :], w_[:, c, n0:n1],
                        start=s, stop=p),
                        (f"h{c}", wr), (f"b{bk}",), tag + f"pj{bk}")
            b.op("vector", lambda E: E.tensor_copy(halo[:], xln[:, :, T:T + 3]),
                 tuple(f"xl{c}" for c in range(NCHK)), ("halo",), tag + "hs")

            def l2norm_block(nm, bank, pk, rr, rres, dstfm, dstfm_res):
                b.op("vector", lambda E, pk=pk: E.tensor_copy(big[:, 0:512], pk[:, 0:512]),
                     (bank,), ("big",), tag + nm + "ev")
                b.op("vector", lambda E: E.tensor_mul(
                    big[:, 512:1024], big[:, 0:512], big[:, 0:512]),
                    ("big",), ("big",), tag + nm + "sq")
                b.op("vector", lambda E, rr=rr: E.reduce_sum(
                    rr[:], big[:, 512:1024].rearrange("p (h d) -> p h d", h=H),
                    axis=AX.X),
                    ("big",), (rres,), tag + nm + "ss")
                b.op("vector", lambda E, rr=rr: E.tensor_scalar(
                    rr[:], rr[:], 1e-24, None, ALU.max),
                    (rres,), (rres,), tag + nm + "mx")
                b.op("scalar", lambda E, rr=rr: E.activation(rr[:], rr[:], ACT.Sqrt),
                     (rres,), (rres,), tag + nm + "sr")
                b.op("vector", lambda E, rr=rr: E.reciprocal(rr[:], rr[:]),
                     (rres,), (rres,), tag + nm + "rc")
                b.op("vector", lambda E, rr=rr: E.tensor_mul(
                    big[:, 512:1024].rearrange("p (h d) -> p h d", h=H),
                    big[:, 0:512].rearrange("p (h d) -> p h d", h=H),
                    rr[:].rearrange("p (h o) -> p h o", o=1).broadcast_to([128, H, 128])),
                    ("big", rres), ("big",), tag + nm + "sc")
                if nm == "k":
                    b.op("vector", lambda E: E.tensor_copy(kt[:, :], big[:, 512:1024]),
                         ("big",), ("kt",), tag + "ktc")
                for hh in range(4):
                    b.op("tensor", lambda E, hh=hh: E.transpose(
                        pb[5][:, hh * 128:(hh + 1) * 128],
                        big[:, 512 + hh * 128:512 + (hh + 1) * 128], ident[:]),
                        ("big", "ident"), ("b5",), tag + nm + "tr")
                b.op("vector", lambda E, dstfm=dstfm: E.tensor_copy(
                    dstfm, pb[5][:, 0:512].rearrange("p (h t) -> p h t", h=H)),
                    ("b5",), (dstfm_res,), tag + nm + "tev")

            l2norm_block("q", "b0", pb[0], rq, "rq", qf[:, :, :], "qf")
            l2norm_block("k", "b1", pb[1], rk, "rk", kf[:, :, :], "kf")
            b.op("scalar", lambda E: E.activation(bt[:], pb[4][:, 0:H], ACT.Tanh,
                                                  scale=0.5),
                 ("b4",), ("bt",), tag + "bsig")
            b.op("vector", lambda E: E.tensor_scalar(
                bt[:], bt[:], 0.5, 0.5, ALU.mult, ALU.add),
                ("bt",), ("bt",), tag + "bsig2")
            b.op("vector", lambda E: E.tensor_copy(vt[:, 0:512], pb[2][:, 0:512]),
                 ("b2",), ("vt",), tag + "vev0")
            b.op("scalar", lambda E: E.activation(vt[:, 512:1024], pb[3][:, 0:512], ACT.Copy),
                 ("b3",), ("vt",), tag + "vev1")
            for hh in range(H):
                b.op("vector", lambda E, hh=hh: E.tensor_scalar(
                    vt[:, hh * 256:(hh + 1) * 256], vt[:, hh * 256:(hh + 1) * 256],
                    bt[:, hh:hh + 1], None, ALU.mult),
                    ("vt", "bt"), ("vt",), tag + "vbf")
            for hh in range(H):
                b.op("vector", lambda E, hh=hh: E.tensor_scalar(
                    big[:, 512 + hh * 128:512 + (hh + 1) * 128],
                    kt[:, hh * 128:(hh + 1) * 128], bt[:, hh:hh + 1],
                    None, ALU.mult),
                    ("kt", "bt"), ("big",), tag + "kbs")
            for hh in range(H):
                b.op("tensor", lambda E, hh=hh: E.transpose(
                    pb[5][:, hh * 128:(hh + 1) * 128],
                    big[:, 512 + hh * 128:512 + (hh + 1) * 128], ident[:]),
                    ("big", "ident"), ("b5",), tag + "kbt")
            b.op("vector", lambda E: E.tensor_copy(big[:, 0:512], pb[5][:, 0:512]),
                 ("b5",), ("big",), tag + "kbe")

            def inversion_prep(hp, bka, bkb, bkx, yoff, ybuf, yres, xbuf, xres):
                h0, h1 = 2 * hp, 2 * hp + 1
                htag = tag + f"ip{hp}_"
                b.op("tensor", lambda E, h0=h0, bkx=bkx: E.matmul(
                    pb[bkx][0:64, 0:64], big[:, h0 * 128:h0 * 128 + 64],
                    kf[:, h0, 0:64],
                    start=True, stop=False), ("kf", "big"), (f"b{bkx}",), htag + "g0")
                b.op("tensor", lambda E, h1=h1, bkx=bkx: E.matmul(
                    pb[bkx][0:64, 128:192], big[:, h1 * 128:h1 * 128 + 64],
                    kf[:, h1, 0:64],
                    start=False, stop=True), ("kf", "big"), (f"b{bkx}",), htag + "g2")
                b.op("tensor", lambda E, h0=h0, bkx=bkx: E.matmul(
                    pb[bkx][64:128, 64:128], big[:, h0 * 128 + 64:h0 * 128 + 128],
                    kf[:, h0, 64:128],
                    start=True, stop=False), ("kf", "big"), (f"b{bkx}",), htag + "g1")
                b.op("tensor", lambda E, h1=h1, bkx=bkx: E.matmul(
                    pb[bkx][64:128, 192:256], big[:, h1 * 128 + 64:h1 * 128 + 128],
                    kf[:, h1, 64:128],
                    start=False, stop=True), ("kf", "big"), (f"b{bkx}",), htag + "g3")
                for (r0, c0) in ((0, 64), (64, 0), (0, 192), (64, 128)):
                    b.op("vector", lambda E, r0=r0, c0=c0, bkx=bkx: E.memset(
                        pb[bkx][r0:r0 + 64, c0:c0 + 64], 0.0),
                        (f"b{bkx}",), (f"b{bkx}",), htag + "gz")
                for yo in (0, 256):
                    b.op("vector", lambda E, yo=yo, yoff=yoff, bkx=bkx, ybuf=ybuf,
                         go=(0 if yo == 0 else 128):
                         E.tensor_mul(ybuf[:, yoff + yo:yoff + yo + 128],
                                      pb[bkx][:, go:go + 128], msl[:]),
                         (f"b{bkx}", "msl"), (yres,), htag + "y0m")
                b.op("tensor", lambda E, yoff=yoff, bkx=bkx, ybuf=ybuf: E.transpose(
                    pb[bkx][:, 0:128], ybuf[:, yoff:yoff + 128], ident[:]),
                    (yres, "ident"), (f"b{bkx}",), htag + "z0t0")
                b.op("tensor", lambda E, yoff=yoff, bkx=bkx, ybuf=ybuf: E.transpose(
                    pb[bkx][:, 128:256], ybuf[:, yoff + 256:yoff + 384], ident[:]),
                    (yres, "ident"), (f"b{bkx}",), htag + "z0t1")
                b.op("vector", lambda E, yoff=yoff, bkx=bkx, ybuf=ybuf: E.tensor_copy(
                    ybuf[:, yoff + 128:yoff + 256], pb[bkx][:, 0:128]),
                    (f"b{bkx}",), (yres,), htag + "z0e0")
                b.op("scalar", lambda E, yoff=yoff, bkx=bkx, ybuf=ybuf: E.activation(
                    ybuf[:, yoff + 384:yoff + 512], pb[bkx][:, 128:256], ACT.Copy),
                    (f"b{bkx}",), (yres,), htag + "z0e1")
                b.op("vector", lambda E, yoff=yoff, xo=0, ybuf=ybuf, xbuf=xbuf: E.tensor_add(
                    xbuf[:, xo:xo + 128], ybuf[:, yoff + 128:yoff + 256], ident[:]),
                    (yres, "ident"), (xres,), htag + "x0a")
                b.op("vector", lambda E, yoff=yoff, xo=0, ybuf=ybuf, xbuf=xbuf: E.tensor_add(
                    xbuf[:, xo + 128:xo + 256], ybuf[:, yoff + 384:yoff + 512], ident[:]),
                    (yres, "ident"), (xres,), htag + "x0b")

            def inversion_level(hp, bka, bkb, bkx, yoff, lev, ybuf, yres, xbuf, xres):
                htag = tag + f"ip{hp}_"
                last = lev == 5
                xo = 0
                b.op("tensor", lambda E, last=last, yoff=yoff, bka=bka, ybuf=ybuf: E.matmul(
                    pb[bka][:, 0:128], ybuf[:, yoff + 128:yoff + 256],
                    ybuf[:, yoff:yoff + 128],
                    start=True, stop=last), (yres,), (f"b{bka}",), htag + f"l{lev}ya")
                if not last:
                    b.op("tensor", lambda E, yoff=yoff, bka=bka, ybuf=ybuf: E.matmul(
                        pb[bka][:, 128:256], ybuf[:, yoff:yoff + 128],
                        ybuf[:, yoff + 128:yoff + 256],
                        start=False, stop=True), (yres,), (f"b{bka}",), htag + f"l{lev}za")
                b.op("tensor", lambda E, last=last, yoff=yoff, bkb=bkb, ybuf=ybuf: E.matmul(
                    pb[bkb][:, 0:128], ybuf[:, yoff + 384:yoff + 512],
                    ybuf[:, yoff + 256:yoff + 384],
                    start=True, stop=last), (yres,), (f"b{bkb}",), htag + f"l{lev}yb")
                if not last:
                    b.op("tensor", lambda E, yoff=yoff, bkb=bkb, ybuf=ybuf: E.matmul(
                        pb[bkb][:, 128:256], ybuf[:, yoff + 256:yoff + 384],
                        ybuf[:, yoff + 384:yoff + 512],
                        start=False, stop=True), (yres,), (f"b{bkb}",), htag + f"l{lev}zb")
                n = 128 if last else 256
                b.op("vector", lambda E, n=n, yoff=yoff, bka=bka, ybuf=ybuf: E.tensor_copy(
                    ybuf[:, yoff:yoff + n], pb[bka][:, 0:n]),
                    (f"b{bka}",), (yres,), htag + f"l{lev}ea")
                b.op("scalar", lambda E, n=n, yoff=yoff, bkb=bkb, ybuf=ybuf: E.activation(
                    ybuf[:, yoff + 256:yoff + 256 + n], pb[bkb][:, 0:n], ACT.Copy),
                    (f"b{bkb}",), (yres,), htag + f"l{lev}eb")
                b.op("tensor", lambda E, yoff=yoff, xo=xo, bkx=bkx, ybuf=ybuf, xbuf=xbuf: E.matmul(
                    pb[bkx][:, 0:128], ybuf[:, yoff:yoff + 128], xbuf[:, xo:xo + 128],
                    start=True, stop=False), (yres, xres), (f"b{bkx}",), htag + f"l{lev}xa")
                b.op("tensor", lambda E, yoff=yoff, xo=xo, bkx=bkx, ybuf=ybuf, xbuf=xbuf: E.matmul(
                    pb[bkx][:, 128:256], ybuf[:, yoff + 256:yoff + 384],
                    xbuf[:, xo + 128:xo + 256],
                    start=False, stop=True), (yres, xres), (f"b{bkx}",), htag + f"l{lev}xb")
                if last:
                    b.op("vector", lambda E, hp=hp, xo=xo, bkx=bkx, xbuf=xbuf: E.tensor_add(
                        tinv[:, 2 * hp:2 * hp + 2, :].rearrange("p a b -> p (a b)"),
                        xbuf[:, xo:xo + 256], pb[bkx][:, 0:256]),
                        (xres, f"b{bkx}"), ("tinv",), htag + "tie")
                else:
                    b.op("vector", lambda E, xo=xo, bkx=bkx, xbuf=xbuf: E.tensor_add(
                        xbuf[:, xo:xo + 256], xbuf[:, xo:xo + 256], pb[bkx][:, 0:256]),
                        (xres, f"b{bkx}"), (xres,), htag + f"l{lev}xe")

            inversion_prep(0, 5, 6, 4, 0, yz, "yz", xv, "xv")
            inversion_prep(1, 2, 3, 7, 0, uu, "uu", mq, "mq")
            for lev in range(1, 6):
                inversion_level(0, 5, 6, 4, 0, lev, yz, "yz", xv, "xv")
                inversion_level(1, 2, 3, 7, 0, lev, uu, "uu", mq, "mq")

        def chain(i, seq, t, c):
            p0 = c * 64
            ctag = f"s{seq}t{t}_c{c}_"
            for hh in range(H):
                b.op("tensor", lambda E, hh=hh, c=c, p0=p0, s=(hh == 0), p=(hh == 3):
                     E.matmul(pb[4][p0:p0 + 64, hh * 64:(hh + 1) * 64],
                              kf[:, hh, c * 64:c * 64 + 64],
                              qf[:, hh, c * 64:c * 64 + 64],
                              start=s, stop=p),
                     ("kf", "qf"), ("b4",), ctag + "mq")
            b.op("vector", lambda E, p0=p0: E.tensor_mul(
                mq[p0:p0 + 64, :].rearrange("p (h j) -> p h j", h=H),
                pb[4][p0:p0 + 64, 0:256].rearrange("p (h j) -> p h j", h=H),
                mui[p0:p0 + 64, :].rearrange("p (o j) -> p o j", o=1)
                .broadcast_to([64, H, 64])),
                ("b4", "mui"), ("mq",), ctag + "mqe")
            for hh in range(H):
                bk = 2 + hh // 2
                co = (hh % 2) * 256
                b.op("tensor", lambda E, hh=hh, c=c, p0=p0, bk=bk, co=co,
                     s=(hh % 2 == 0), p=(hh % 2 == 1): E.matmul(
                    pb[bk][p0:p0 + 64, co:co + 256],
                    big[:, hh * 128 + p0:hh * 128 + p0 + 64], S[:, hh, :],
                    start=s, stop=p),
                    ("big", f"S{hh}"), (f"b{bk}",), ctag + "w1")
            for hh in range(H):
                bk = 5 + hh // 2
                co = (hh % 2) * 256
                b.op("tensor", lambda E, hh=hh, c=c, p0=p0, bk=bk, co=co,
                     s=(hh % 2 == 0): E.matmul(
                    pb[bk][p0:p0 + 64, co:co + 256],
                    qf[:, hh, c * 64:c * 64 + 64], S[:, hh, :],
                    start=s, stop=False),
                    ("qf", f"S{hh}"), (f"b{bk}",), ctag + "o1")
            for bk in (2, 3):
                hh0 = (bk - 2) * 2
                b.op("vector", lambda E, bk=bk, hh0=hh0, p0=p0: E.tensor_sub(
                    wp[p0:p0 + 64, hh0 * 256:(hh0 + 2) * 256],
                    vt[p0:p0 + 64, hh0 * 256:(hh0 + 2) * 256],
                    pb[bk][p0:p0 + 64, 0:512]),
                    ("vt", f"b{bk}"), ("wp",), ctag + "wpa")
            for hh in range(H):
                bk = 2 + hh // 2
                co = (hh % 2) * 256
                b.op("tensor", lambda E, hh=hh, p0=p0, bk=bk, co=co: E.matmul(
                    pb[bk][p0:p0 + 64, co:co + 256],
                    tinv[p0:p0 + 64, hh, p0:p0 + 64],
                    wp[p0:p0 + 64, hh * 256:(hh + 1) * 256],
                    start=(hh % 2 == 0), stop=(hh % 2 == 1)),
                    ("tinv", "wp"), (f"b{bk}",), ctag + "u")
            b.op("vector", lambda E, p0=p0: E.tensor_copy(
                uu[p0:p0 + 64, 0:512], pb[2][p0:p0 + 64, 0:512]),
                ("b2",), ("uu",), ctag + "uea")
            b.op("scalar", lambda E, p0=p0: E.activation(
                uu[p0:p0 + 64, 512:1024], pb[3][p0:p0 + 64, 0:512], ACT.Copy),
                ("b3",), ("uu",), ctag + "ueb")
            for hh in range(H):
                bk = 5 + hh // 2
                co = (hh % 2) * 256
                b.op("tensor", lambda E, hh=hh, p0=p0, bk=bk, co=co,
                     p=(hh % 2 == 1): E.matmul(
                    pb[bk][p0:p0 + 64, co:co + 256],
                    mq[p0:p0 + 64, hh * 64:(hh + 1) * 64],
                    uu[p0:p0 + 64, hh * 256:(hh + 1) * 256],
                    start=False, stop=p),
                    ("mq", "uu"), (f"b{bk}",), ctag + "o2")
            b.op("vector", lambda E, p0=p0: E.tensor_copy(
                ot[p0:p0 + 64, 0:512], pb[5][p0:p0 + 64, 0:512]),
                ("b5",), ("ot",), ctag + "oea")
            b.op("scalar", lambda E, p0=p0: E.activation(
                ot[p0:p0 + 64, 512:1024], pb[6][p0:p0 + 64, 0:512], ACT.Copy),
                ("b6",), ("ot",), ctag + "oeb")
            for hh in range(H):
                bk = 2 + hh // 2
                co = (hh % 2) * 256
                b.op("tensor", lambda E, hh=hh, p0=p0, bk=bk, co=co: E.matmul(
                    pb[bk][:, co:co + 256],
                    kt[p0:p0 + 64, hh * 128:(hh + 1) * 128],
                    uu[p0:p0 + 64, hh * 256:(hh + 1) * 256],
                    start=(hh % 2 == 0), stop=(hh % 2 == 1)),
                    ("kt", "uu"), (f"b{bk}",), ctag + "ds")
            b.op("vector", lambda E: E.tensor_add(
                S[:, 0:2, :].rearrange("p a b -> p (a b)"),
                S[:, 0:2, :].rearrange("p a b -> p (a b)"), pb[2][:, 0:512]),
                ("S0", "S1", "b2"), ("S0", "S1"), ctag + "sua")
            b.op("vector", lambda E: E.tensor_add(
                S[:, 2:4, :].rearrange("p a b -> p (a b)"),
                S[:, 2:4, :].rearrange("p a b -> p (a b)"), pb[3][:, 0:512]),
                ("S2", "S3", "b3"), ("S2", "S3"), ctag + "sub")

        def tile_back(i, seq, t, nxt):
            par = i % 2
            x = x2_[par]
            tag = f"s{seq}t{t}_"
            gtag = tag + "g_"
            chain(i, seq, t, 0)
            for c in range(NCHK):
                for half in range(2):
                    b.op("tensor", lambda E, c=c, half=half,
                         s=(c == 0), p=(c == NCHK - 1): E.matmul(
                        pb[half][:, 0:512], h[:, c, :],
                        wg[:, c, half * 512:(half + 1) * 512],
                        start=s, stop=p),
                        (f"h{c}", "wg"), (f"b{half}",), tag + "gmm")
            chain(i, seq, t, 1)
            for half in range(2):
                b.op("scalar", lambda E, half=half: E.activation(
                    big[:, half * 512:(half + 1) * 512], pb[half][:, 0:512], ACT.Silu),
                    (f"b{half}",), ("big",), gtag + "gsil")
            for half in range(2):
                bk = 6 + half
                b.op("scalar", lambda E, half=half, bk=bk: E.activation(
                    pb[bk][:, 0:512], ot[:, half * 512:(half + 1) * 512], ACT.Square),
                    ("ot",), (f"b{bk}",), gtag + "osq")
                b.op("vector", lambda E, half=half, bk=bk: E.reduce_sum(
                    rq[:, half * 2:half * 2 + 2],
                    pb[bk][:, 0:512].rearrange("p (h d) -> p h d", h=2), axis=AX.X),
                    (f"b{bk}",), ("rq",), gtag + "ored")
            b.op("vector", lambda E: E.tensor_scalar(
                rq[:], rq[:], 1.0 / DVH, NORM_EPS, ALU.mult, ALU.add),
                ("rq",), ("rq",), gtag + "eps")
            b.op("scalar", lambda E: E.activation(rq[:], rq[:], ACT.Sqrt),
                 ("rq",), ("rq",), gtag + "sqrt")
            b.op("vector", lambda E: E.reciprocal(rq[:], rq[:]),
                 ("rq",), ("rq",), gtag + "rc")
            if nxt is not None:
                ni, nseq, ntt = nxt
                ntag = f"s{nseq}t{ntt}_"
                nsrc = x_h[nseq].rearrange("(c p) t -> p c t", p=128)[:, :, ntt * T:(ntt + 1) * T]
                dma("sync", x2_[ni % 2][:], nsrc,
                    (), tuple(f"x{ni % 2}_{c}" for c in range(NCHK)), ntag + "ldx")
                ffn_negs(ni, ntag + "ffn1_")
                ffn_wave(ni, (0, 1, 2, 3), (0, 1, 2, 3), ntag + "ffn1_a")
            b.op("vector", lambda E: E.tensor_mul(
                vt[:].rearrange("p (h d) -> p h d", h=H),
                ot[:].rearrange("p (h d) -> p h d", h=H),
                rq[:].rearrange("p (h o) -> p h o", o=1).broadcast_to([128, H, DVH])),
                ("ot", "rq"), ("vt",), gtag + "rms")
            b.op("vector", lambda E: E.tensor_mul(ot[:], vt[:], big[:]),
                 ("vt", "big"), ("ot",), gtag + "gate")
            b.op("vector", lambda E: E.tensor_mul(
                vt[:].rearrange("p (h d) -> p h d", h=H),
                ot[:].rearrange("p (h d) -> p h d", h=H),
                nwb[:].rearrange("p (o d) -> p o d", o=1).broadcast_to([128, H, DVH])),
                ("ot", "nwb"), ("vt",), gtag + "nw")
            for j in range(8):
                bk = 6 if j < 4 else 7
                b.op("tensor", lambda E, j=j, bk=bk: E.transpose(
                    pb[bk][:, (j % 4) * 128:(j % 4 + 1) * 128],
                    vt[:, j * 128:(j + 1) * 128], ident[:]),
                    ("vt", "ident"), (f"b{bk}",), gtag + "otr")
            b.op("vector", lambda E: E.tensor_copy(
                wp[:, 0:512], pb[6][:, 0:512]),
                ("b6",), ("wp",), gtag + "oev0")
            b.op("scalar", lambda E: E.activation(
                wp[:, 512:1024], pb[7][:, 0:512], ACT.Copy),
                ("b7",), ("wp",), gtag + "oev1")
            for m in range(8):
                bk = 4 + m % 2
                for k in range(8):
                    b.op("tensor", lambda E, k=k, m=m, bk=bk, s=(k == 0), p=(k == 7):
                         E.matmul(pb[bk][:, 0:T], wo[:, k, m * 128:(m + 1) * 128],
                                  wp[:, k * 128:(k + 1) * 128], start=s, stop=p),
                         ("wo", "wp"), (f"b{bk}",), gtag + "womm")
                b.op("vector", lambda E, m=m, bk=bk, x=x: E.tensor_add(
                    x[:, m, :], x[:, m, :], pb[bk][:, 0:T]),
                    (f"x{par}_{m}", f"b{bk}"), (f"x{par}_{m}",), gtag + "woadd")
            if nxt is not None:
                ln_stats_mms(ni, 6, ntag + "ln1_")
                ln_stats_smalls(ni, 6, ntag + "ln1_")
                ln_stats_mms(i, 7, tag + "ln3_")
                ln_stats_fin(ni, 6, ntag + "ln1_", "fold")
                ffn_wave(ni, (4, 5, 6), (4, 5, 6), ntag + "ffn1_b")
                ln_stats_smalls(i, 7, tag + "ln3_")
                ln_stats_fin(i, 7, tag + "ln3_", "fold")
                ffn_wave(ni, (7,), (7,), ntag + "ffn1_c")
                ffn_epi(ni, (0, 1, 2, 3), (0, 1, 2, 3), ntag + "ffn1_a")
                ffn_epi(ni, (4, 5, 6), (4, 5, 6), ntag + "ffn1_b")
                ffn_epi(ni, (7,), (7,), ntag + "ffn1_c")
            else:
                ln_stats(i, 7, tag + "ln3_", "fold")
            ffn_negs(i, tag + "ffn2_")
            ffn_wave(i, (0, 1, 2, 3), (0, 1, 2, 3), tag + "ffn2_a")
            ffn_wave(i, (4, 5, 6, 7), (4, 5, 6, 7), tag + "ffn2_b")
            ffn_epi(i, (0, 1, 2, 3), (0, 1, 2, 3), tag + "ffn2_a")
            ffn_epi(i, (4, 5, 6, 7), (4, 5, 6, 7), tag + "ffn2_b")
            dstv = y_h[seq].rearrange("(c p) t -> p c t", p=128)[:, :, t * T:(t + 1) * T]
            b.op("scalar", lambda E, d=dstv, x=x: E.dma_start(out=d, in_=x[:]),
                 tuple(f"x{par}_{c}" for c in range(NCHK)), (), dma=True,
                 tag=tag + "st", force=True)

        tiles = [(i, i // NTILE, i % NTILE) for i in range(BPC * NTILE)]
        ln_stats(0, 7, "s0t0_ln1_", "fold")
        ffn_negs(0, "s0t0_ffn1_")
        ffn_wave(0, (0, 1, 2, 3), (0, 1, 2, 3), "s0t0_ffn1_a")
        ffn_wave(0, (4, 5, 6, 7), (4, 5, 6, 7), "s0t0_ffn1_b")
        ffn_epi(0, (0, 1, 2, 3), (0, 1, 2, 3), "s0t0_ffn1_a")
        ffn_epi(0, (4, 5, 6, 7), (4, 5, 6, 7), "s0t0_ffn1_b")
        for hh in range(H):
            b.op("gpsimd", lambda E, hh=hh: E.memset(S[:, hh, :], 0.0),
                 (), (f"S{hh}",), "s0_szero")
        for idx, (i, seq, t) in enumerate(tiles):
            if t == 0 and seq > 0:
                for hh in range(H):
                    b.op("gpsimd", lambda E, hh=hh: E.memset(S[:, hh, :], 0.0),
                         tuple(), (f"S{hh}",), f"s{seq}_szero")
            tile_front(i, seq, t)
            nxt = None
            if idx + 1 < len(tiles):
                nxt = tiles[idx + 1]
            tile_back(i, seq, t, nxt)

        b.finalize()
        out_cum = max(o.cum for o in b.plan["scalar"] if o.dma)

        with nc.Block() as block:
            @block.sync
            def _(eng):
                b.replay("sync", eng, sems)

            @block.tensor
            def _(eng):
                b.replay("tensor", eng, sems)

            @block.vector
            def _(eng):
                b.replay("vector", eng, sems)

            @block.scalar
            def _(eng):
                b.replay("scalar", eng, sems)
                eng.wait_ge(sems["scalar"][1], out_cum)

            @block.gpsimd
            def _(eng):
                b.replay("gpsimd", eng, sems)

    return nc


def _host_stage(inputs):
    xs = np.asarray(inputs["L0_attn_hidden_states"], np.float32)
    G1R = np.asarray(inputs["L0_ffn_G1_R"], np.float32)
    G2R = np.asarray(inputs["L0_ffn_G2_R"], np.float32)
    G1I = np.asarray(inputs["L0_ffn_G1_I"], np.float32)
    G2I = np.asarray(inputs["L0_ffn_G2_I"], np.float32)
    acr = np.zeros((128, 2, 4, 4, 32), np.float32)
    bcr = np.zeros((128, 2, 4, 16), np.float32)
    p_idx = np.arange(128)
    for ri, (G1, G2) in enumerate(((G1R, G2R), (G1I, G2I))):
        for c in range(4):
            for kc in range(4):
                d_idx = (kc * 128 + p_idx) // 16
                acr[:, ri, c, kc, :] = G1[0, :, c, :].T[d_idx, :]
            g_idx = p_idx % 16
            bcr[:, ri, c, :] = G2[c, :, 0, :].T[g_idx, :]
    biasri = np.concatenate([np.asarray(inputs["L0_ffn_bias_real"], np.float32),
                             np.asarray(inputs["L0_ffn_bias_imag"], np.float32)])
    biasri = np.ascontiguousarray(biasri.reshape(8, 128).T)
    nw = np.ascontiguousarray(np.asarray(inputs["L0_attn_norm_w"],
                                         np.float32).reshape(1, 256))
    shared = {
        "wq": np.ascontiguousarray(inputs["L0_attn_Wq"], np.float32),
        "wk": np.ascontiguousarray(inputs["L0_attn_Wk"], np.float32),
        "wv": np.ascontiguousarray(inputs["L0_attn_Wv"], np.float32),
        "wg": np.ascontiguousarray(inputs["L0_attn_Wg"], np.float32),
        "wo": np.ascontiguousarray(inputs["L0_attn_Wo"], np.float32),
        "wb": np.ascontiguousarray(inputs["L0_attn_Wb"], np.float32),
        "cw": np.ascontiguousarray(inputs["L0_attn_conv_w"], np.float32),
        "acr": acr, "bcr": bcr, "biasri": biasri, "normw": nw,
    }
    in_maps = []
    for core in range(NCORE):
        xc = np.ascontiguousarray(
            xs[core * BPC:(core + 1) * BPC].transpose(0, 2, 1))
        m = dict(shared)
        m["x_in"] = xc
        in_maps.append(m)
    return in_maps


def kernel(**inputs):
    in_maps = _host_stage(inputs)
    nc = build_program()
    res = run_bass_kernel_spmd(nc, in_maps, core_ids=list(range(NCORE)))
    outs = []
    for core in range(NCORE):
        yc = np.asarray(res.results[core]["y_out"])
        outs.append(yc.transpose(0, 2, 1))
    return np.ascontiguousarray(np.concatenate(outs, axis=0))


# version 22
# speedup vs baseline: 1.0258x; 1.0258x over previous
import numpy as np
import concourse.bass as bass
import concourse.mybir as mybir
from concourse.bass_utils import run_bass_kernel_spmd

F32 = mybir.dt.float32
AX = mybir.AxisListType
ALU = mybir.AluOpType
ACT = mybir.ActivationFunctionType

NCORE = 8
DBG = False
PARANOID = False
PARANOID_ENGS = ("vector",)
BPC = 2
L = 2048
D = 1024
T = 128
NTILE = L // T
NCHK = D // 128
H = 4
DQH = 128
DVH = 256
CH = 64
NCH = T // CH
LN_EPS = 1e-5
NORM_EPS = 1e-5
ENGS = ("sync", "tensor", "vector", "scalar", "gpsimd")
PROF_ANNOTATE = False


class Op:
    __slots__ = ("eng", "fn", "deps", "dma", "needed", "pos", "cum", "tag")

    def __init__(self, eng, fn, dma, tag):
        self.eng = eng
        self.fn = fn
        self.dma = dma
        self.needed = False
        self.deps = []
        self.pos = -1
        self.cum = None
        self.tag = tag


class Builder:
    def __init__(self):
        self.plan = {e: [] for e in ENGS}
        self.res = {}

    def op(self, eng, fn, reads=(), writes=(), tag="", dma=False, force=False):
        o = Op(eng, fn, dma, tag)
        if force or dma:
            o.needed = True
        o.pos = len(self.plan[eng])
        deps = {}

        def dep(p):
            if p is None:
                return
            if (p.eng == eng and eng != "gpsimd" and not p.dma
                    and not PARANOID and eng not in PARANOID_ENGS):
                return
            cur = deps.get(p.eng)
            if cur is None or p.pos > cur.pos:
                deps[p.eng] = p

        for r in reads:
            ent = self.res.get(r)
            if ent is not None:
                dep(ent[0])
        for w in writes:
            ent = self.res.get(w)
            if ent is not None:
                dep(ent[0])
                for rd in ent[1]:
                    dep(rd)
        o.deps = list(deps.values())
        for p in o.deps:
            p.needed = True
        for r in reads:
            self.res.setdefault(r, [None, []])[1].append(o)
        for w in writes:
            self.res[w] = [o, []]
        self.plan[eng].append(o)
        return o

    def finalize(self):
        for e in ENGS:
            cum_c = 0
            cum_d = 0
            for o in self.plan[e]:
                if o.needed:
                    if o.dma:
                        cum_d += 16
                        o.cum = cum_d
                    else:
                        cum_c += 1
                        o.cum = cum_c

    def replay(self, eng_name, eng, sems, dma_window=1):
        csem, dsem = sems[eng_name]
        ndma = 0
        for o in self.plan[eng_name]:
            if o.dma:
                ndma += 1
                if ndma > dma_window:
                    eng.wait_ge(dsem, 16 * (ndma - dma_window))
            waitv = {}
            for p in o.deps:
                if p.eng == eng_name and p.pos >= o.pos:
                    raise RuntimeError("self-dep ordering bug")
                key = (p.eng, p.dma)
                v = waitv.get(key)
                if v is None or p.cum > v:
                    waitv[key] = p.cum
            for (pe, pdma), v in waitv.items():
                s = sems[pe][1 if pdma else 0]
                eng.wait_ge(s, v)
            inst = o.fn(eng)
            if PROF_ANNOTATE and o.tag:
                inst.annotate(o.tag)
            if o.needed:
                inst.then_inc(dsem if o.dma else csem, 16 if o.dma else 1)


def build_program():
    nc = bass.Bass("TRN2", target_bir_lowering=False, detect_race_conditions=False)
    x_h = nc.dram_tensor("x_in", [BPC, D, L], F32, kind="ExternalInput").ap()
    y_h = nc.dram_tensor("y_out", [BPC, D, L], F32, kind="ExternalOutput").ap()
    wq_h = nc.dram_tensor("wq", [D, 512], F32, kind="ExternalInput").ap()
    wk_h = nc.dram_tensor("wk", [D, 512], F32, kind="ExternalInput").ap()
    wv_h = nc.dram_tensor("wv", [D, 1024], F32, kind="ExternalInput").ap()
    wg_h = nc.dram_tensor("wg", [D, 1024], F32, kind="ExternalInput").ap()
    wo_h = nc.dram_tensor("wo", [1024, D], F32, kind="ExternalInput").ap()
    wb_h = nc.dram_tensor("wb", [D, H], F32, kind="ExternalInput").ap()
    cw_h = nc.dram_tensor("cw", [D, 4], F32, kind="ExternalInput").ap()
    acr_h = nc.dram_tensor("acr", [128, 2, 4, 4, 32], F32, kind="ExternalInput").ap()
    bcr_h = nc.dram_tensor("bcr", [128, 2, 4, 16], F32, kind="ExternalInput").ap()
    bias_h = nc.dram_tensor("biasri", [128, 8], F32, kind="ExternalInput").ap()
    nw_h = nc.dram_tensor("normw", [1, 256], F32, kind="ExternalInput").ap()

    dbg_tensors = {}

    def declare_dbg(name, shape):
        dbg_tensors[name] = nc.dram_tensor("dbg_" + name, list(shape), F32,
                                           kind="ExternalOutput").ap()
        return dbg_tensors[name]

    import contextlib
    ctx = contextlib.ExitStack()
    with ctx:
        sb = lambda name, shape: ctx.enter_context(nc.sbuf_tensor(name, shape, F32))
        wq = sb("wq_s", [128, NCHK, 512])
        wk = sb("wk_s", [128, NCHK, 512])
        wv = sb("wv_s", [128, NCHK, 1024])
        wg = sb("wg_s", [128, NCHK, 1024])
        wo = sb("wo_s", [128, NCHK, 1024])
        wb = sb("wb_s", [128, NCHK, H])
        cw = sb("cw_s", [128, NCHK, 4])
        wtr = sb("wtr_s", [128, 4, 512])
        wti = sb("wti_s", [128, 4, 512])
        bias = sb("bias_s", [128, 8])
        nw1 = sb("nw1_s", [1, 256])
        nwb = sb("nwb_s", [128, 256])
        ones_c = sb("ones_c", [128, 1])
        ones_r = sb("ones_r", [1, 128])
        ident = sb("ident_s", [128, 128])
        msl = sb("msl_s", [128, 128])
        mui = sb("mui_s", [128, 64])
        x2_ = [sb("xa_s", [128, NCHK, T]), sb("xb_s", [128, NCHK, T])]
        xln = sb("xln_s", [128, NCHK, T + 3])
        h = sb("h_s", [128, NCHK, T])
        qf = sb("qf_s", [128, H, T])
        kf = sb("kf_s", [128, H, T])
        kt = sb("kt_s", [128, 512])
        vt = sb("vt_s", [128, 1024])
        ot = sb("ot_s", [128, 1024])

        S = sb("S_s", [128, H, DVH])
        tinv = sb("tinv_s", [128, H, 128])
        yz = sb("yz_s", [128, 512])
        xv = sb("xv_s", [128, 256])
        mq = sb("mq_s", [128, 256])
        wp = sb("wp_s", [128, 1024])
        uu = sb("uu_s", [128, 1024])
        cbc2_ = [sb("cbca_s", [128, 256]), sb("cbcb_s", [128, 256])]
        csm = sb("csm_s", [128, 8])
        halfb = sb("halfb_s", [128, 8])
        negone = sb("negone_s", [128, 1])
        bt = sb("bt_s", [128, H])
        nbt = sb("nbt_s", [128, H])
        rq = sb("rq_s", [128, H])
        rk = sb("rk_s", [128, H])
        big = sb("big_s", [128, 1024])
        tmp = sb("tmp_s", [128, 512])
        halo = sb("halo_s", [128, NCHK, 3])
        st1 = sb("st1_s", [1, 256])
        st2 = sb("st2_s", [1, 256])
        st3 = sb("st3_s", [1, 256])
        pb = [ctx.enter_context(nc.psum_tensor(f"pb{i}", [128, 512], F32))
              for i in range(8)]

        sems = {}
        for e in ENGS:
            sems[e] = (ctx.enter_context(nc.semaphore(f"c_{e}")),
                       ctx.enter_context(nc.semaphore(f"d_{e}")))

        b = Builder()

        def dump(name, ap, reads):
            if not DBG:
                return
            d = declare_dbg(name, ap.shape)
            b.op("scalar", lambda E, d=d, a=ap: E.dma_start(out=d, in_=a),
                 reads, (), dma=True, force=True, tag="dbg_" + name)

        def dma(eng, dst, src, reads, writes, tag):
            b.op(eng, lambda E, d=dst, s=src: E.dma_start(out=d, in_=s),
                 reads=reads, writes=writes, dma=True, tag=tag)

        b.op("gpsimd", lambda E: E.memset(ones_c[:], 1.0), (), ("ones_c",))
        b.op("gpsimd", lambda E: E.memset(ones_r[:], 1.0), (), ("ones_r",))
        b.op("gpsimd", lambda E: E.memset(ident[:], 0.0), (), ("ident",))
        b.op("gpsimd", lambda E: E.affine_select(
            out=ident[:], in_=ident[:], compare_op=ALU.not_equal, fill=1.0,
            base=0, pattern=[[-1, 128]], channel_multiplier=1),
            ("ident",), ("ident",))
        b.op("gpsimd", lambda E: E.memset(msl[:], 1.0), (), ("msl",))
        b.op("gpsimd", lambda E: E.affine_select(
            out=msl[0:64, :], in_=msl[0:64, :], compare_op=ALU.is_ge, fill=0.0,
            base=-1, pattern=[[-1, 128]], channel_multiplier=1),
            ("msl",), ("msl",))
        b.op("gpsimd", lambda E: E.affine_select(
            out=msl[64:128, :], in_=msl[64:128, :], compare_op=ALU.is_ge, fill=0.0,
            base=63, pattern=[[-1, 128]], channel_multiplier=1),
            ("msl",), ("msl",))
        b.op("gpsimd", lambda E: E.affine_select(
            out=msl[64:128, :], in_=msl[64:128, :], compare_op=ALU.is_ge, fill=0.0,
            base=-64, pattern=[[1, 128]], channel_multiplier=0),
            ("msl",), ("msl",))
        b.op("gpsimd", lambda E: E.memset(mui[:], 1.0), (), ("mui",))
        b.op("gpsimd", lambda E: E.affine_select(
            out=mui[0:64, :], in_=mui[0:64, :], compare_op=ALU.is_ge, fill=0.0,
            base=0, pattern=[[1, 64]], channel_multiplier=-1),
            ("mui",), ("mui",))
        b.op("gpsimd", lambda E: E.affine_select(
            out=mui[64:128, :], in_=mui[64:128, :], compare_op=ALU.is_ge, fill=0.0,
            base=0, pattern=[[1, 64]], channel_multiplier=-1),
            ("mui",), ("mui",))

        dma("sync", big[:].rearrange("p (r c k b) -> p r c k b", r=2, c=4, k=4),
            acr_h, (), ("big",), "ldacr")
        dma("sync", tmp[:, 0:128].rearrange("p (r c e) -> p r c e", r=2, c=4),
            bcr_h, (), ("tmp0",), "ldbcr")
        dma("sync", bias[:], bias_h, (), ("bias",), "ldbias")
        dma("sync", nw1[:], nw_h, (), ("nw1",), "ldnw")
        dma("sync", cw[:], cw_h.rearrange("(k p) n -> p k n", p=128), (), ("cw",), "ldcw")
        dma("sync", wb[:], wb_h.rearrange("(k p) n -> p k n", p=128), (), ("wb",), "ldwb")
        dma("scalar", wq[:], wq_h.rearrange("(k p) n -> p k n", p=128), (), ("wq",), "ldwq")
        dma("scalar", wk[:], wk_h.rearrange("(k p) n -> p k n", p=128), (), ("wk",), "ldwk")
        dma("scalar", wv[:], wv_h.rearrange("(k p) n -> p k n", p=128), (), ("wv",), "ldwv")
        dma("gpsimd", wg[:], wg_h.rearrange("(k p) n -> p k n", p=128), (), ("wg",), "ldwg")
        dma("gpsimd", wo[:], wo_h.rearrange("(k p) n -> p k n", p=128), (), ("wo",), "ldwo")

        b.op("tensor", lambda E: E.matmul(pb[7][:, 0:256], ones_r[:], nw1[:],
                                          start=True, stop=True),
             ("ones_r", "nw1"), ("b7",))
        b.op("vector", lambda E: E.tensor_copy(nwb[:], pb[7][:, 0:256]),
             ("b7",), ("nwb",))

        acr_v = big[:].rearrange("p (r c k b) -> p r c k b", r=2, c=4, k=4)
        bcr_v = tmp[:, 0:128].rearrange("p (r c e) -> p r c e", r=2, c=4)
        for ri, (wt, wname) in enumerate(((wtr, "wtr"), (wti, "wti"))):
            for kc in range(4):
                for c in range(4):
                    a_ap = acr_v[:, ri, c, kc, :].rearrange(
                        "p (b o) -> p b o", o=1).broadcast_to([128, 32, 16])
                    b_ap = bcr_v[:, ri, c, :].rearrange(
                        "p (o e) -> p o e", o=1).broadcast_to([128, 32, 16])
                    dst = wt[:, kc, :].rearrange("p (b e) -> p b e", e=16)
                    if c == 0:
                        b.op("vector",
                             lambda E, a=a_ap, bb=b_ap, d=dst: E.tensor_mul(d, a, bb),
                             ("big", "tmp0"), (wname,))
                    else:
                        tview = uu[:, 0:512].rearrange("p (b e) -> p b e", e=16)
                        b.op("vector",
                             lambda E, a=a_ap, bb=b_ap, d=tview: E.tensor_mul(d, a, bb),
                             ("big", "tmp0"), ("uu",))
                        b.op("vector",
                             lambda E, d=dst, t=tview: E.tensor_add(d, d, t),
                             ("uu", wname), (wname,))

        b.op("gpsimd", lambda E: E.memset(negone[:], -1.0), (), ("negone",))
        b.op("vector", lambda E: E.tensor_scalar(msl[:], msl[:], -1.0, None, ALU.mult),
             ("msl",), ("msl",), "mslneg")
        first_cs = [True]
        for m in range(8):
            ri = 0 if m < 4 else 1
            mo = m % 4
            for k in range(4):
                st = first_cs[0]; first_cs[0] = False
                w1 = wtr if ri == 0 else wti
                r1 = "wtr" if ri == 0 else "wti"
                b.op("tensor", lambda E, w1=w1, k=k, mo=mo, m=m, st=st: E.matmul(
                    pb[6][:, m:m + 1], w1[:, k, mo * 128:(mo + 1) * 128],
                    ones_c[:], start=st, stop=False),
                    (r1, "ones_c"), ("b6",), "cs1")
            for k in range(4):
                w2 = wti if ri == 0 else wtr
                r2 = "wti" if ri == 0 else "wtr"
                o2_ = negone if ri == 0 else ones_c
                or2 = "negone" if ri == 0 else "ones_c"
                b.op("tensor", lambda E, w2=w2, k=k, mo=mo, m=m,
                     o2_=o2_, last=(m == 7 and k == 3): E.matmul(
                    pb[6][:, m:m + 1], w2[:, k, mo * 128:(mo + 1) * 128],
                    o2_[:], start=False, stop=last),
                    (r2, or2), ("b6",), "cs2")
        b.op("vector", lambda E: E.tensor_copy(csm[:], pb[6][:, 0:8]),
             ("b6",), ("csm",), "csev")
        b.op("vector", lambda E: E.tensor_scalar(halfb[:], bias[:], 0.5, None, ALU.mult),
             ("bias",), ("halfb",), "hbv")

        def ln_stats_mms(i, bank, tag):
            par = i % 2
            x = x2_[par]
            pk = pb[bank]
            for c in range(NCHK):
                b.op("tensor",
                     lambda E, c=c, x=x, pk=pk, s=(c == 0): E.matmul(
                         pk[0:1, 0:T], ones_c[:], x[:, c, :], start=s, stop=False),
                     ("ones_c", f"x{par}_{c}"), (f"b{bank}",), tag=tag + "sum")
            for c in range(NCHK):
                slot = (c % 2 + 2 * par) * T
                b.op("scalar",
                     lambda E, c=c, x=x, slot=slot: E.activation(
                         tmp[:, slot:slot + T], x[:, c, :], ACT.Square),
                     (f"x{par}_{c}",), (f"tmp{slot // T}",), tag=tag + "sq")
                b.op("tensor",
                     lambda E, pk=pk, slot=slot, last=(c == NCHK - 1): E.matmul(
                         pk[0:1, T:2 * T], ones_c[:], tmp[:, slot:slot + T],
                         start=False, stop=last),
                     ("ones_c", f"tmp{slot // T}"), (f"b{bank}",), tag=tag + "ssq")

        def ln_stats_smalls(i, bank, tag):
            pk = pb[bank]
            b.op("vector", lambda E, pk=pk: E.tensor_scalar(
                st1[:, 0:T], pk[0:1, 0:T], 1.0 / D, None, ALU.mult),
                (f"b{bank}",), ("st1",), tag=tag + "mu")
            b.op("vector", lambda E, pk=pk: E.tensor_scalar(
                st2[:, 0:T], pk[0:1, T:2 * T], 1.0 / D, None, ALU.mult),
                (f"b{bank}",), ("st2",), tag=tag + "ms")
            b.op("vector", lambda E: E.tensor_mul(
                st3[:, 0:T], st1[:, 0:T], st1[:, 0:T]),
                ("st1",), ("st3",), tag=tag + "musq")
            b.op("vector", lambda E: E.tensor_sub(
                st2[:, 0:T], st2[:, 0:T], st3[:, 0:T]),
                ("st2", "st3"), ("st2",), tag=tag + "var")
            b.op("vector", lambda E: E.tensor_scalar(
                st2[:, 0:T], st2[:, 0:T], LN_EPS, None, ALU.add),
                ("st2",), ("st2",), tag=tag + "eps")
            b.op("scalar", lambda E: E.activation(st2[:, 0:T], st2[:, 0:T], ACT.Sqrt),
                 ("st2",), ("st2",), tag=tag + "sqrt")
            b.op("vector", lambda E: E.reciprocal(st2[:, 0:T], st2[:, 0:T]),
                 ("st2",), ("st2",), tag=tag + "rc")

        def ln_stats_fin(i, bank, tag, mode):
            par = i % 2
            pk = pb[bank]
            if mode == "fold":
                b.op("vector", lambda E: E.tensor_scalar(
                    st1[:, 0:T], st1[:, 0:T], -1.0, None, ALU.mult),
                    ("st1",), ("st1",), tag=tag + "c2")
                b.op("vector", lambda E: E.tensor_scalar(
                    st2[:, 0:T], st2[:, 0:T], 0.5, None, ALU.mult),
                    ("st2",), ("st2",), tag=tag + "c1")
            b.op("tensor", lambda E, pk=pk: E.matmul(pk[:, 0:T], ones_r[:], st2[:, 0:T],
                                                     start=True, stop=False),
                 ("ones_r", "st2"), (f"b{bank}",), tag=tag + "bc1")
            b.op("tensor", lambda E, pk=pk: E.matmul(pk[:, T:2 * T], ones_r[:], st1[:, 0:T],
                                                     start=False, stop=True),
                 ("ones_r", "st1"), (f"b{bank}",), tag=tag + "bc2")
            b.op("vector", lambda E, pk=pk, cb=cbc2_[par]: E.tensor_copy(
                cb[:, 0:2 * T], pk[:, 0:2 * T]),
                (f"b{bank}",), (f"cbc{par}",), tag=tag + "cbev")

        def ln_stats(i, bank, tag, mode):
            ln_stats_mms(i, bank, tag)
            ln_stats_smalls(i, bank, tag)
            ln_stats_fin(i, bank, tag, mode)

        def ffn_negs(i, tag):
            par = i % 2
            x = x2_[par]
            for k in range(4, 8):
                b.op("vector", lambda E, k=k, x=x: E.tensor_scalar(
                    uu[:, (k - 4) * T:(k - 3) * T], x[:, k, :], -1.0, None, ALU.mult),
                    (f"x{par}_{k}",), ("uu",), tag=tag + "neg")

        def ffn_wave(i, ms, banks, tag):
            par = i % 2
            x = x2_[par]
            for k in range(8):
                for m, bk in zip(ms, banks):
                    ri = 0 if m < 4 else 1
                    mo = m % 4
                    if ri == 0:
                        wt_, nm_ = (wtr, "wtr") if k < 4 else (wti, "wti")
                    else:
                        wt_, nm_ = (wti, "wti") if k < 4 else (wtr, "wtr")
                    lhs = wt_[:, k % 4, mo * 128:(mo + 1) * 128]
                    if ri == 0 and k >= 4:
                        rhs, rr = uu[:, (k - 4) * T:(k - 3) * T], ("uu",)
                    else:
                        rhs, rr = x[:, k, :], (f"x{par}_{k}",)
                    b.op("tensor",
                         lambda E, l=lhs, r=rhs, bk=bk, s=(k == 0), p=(k == 7):
                         E.matmul(pb[bk][:, 0:T], l, r, start=s, stop=p),
                         (nm_,) + rr, (f"b{bk}",), tag=tag + "mm")

        def ffn_epi(i, ms, banks, tag):
            par = i % 2
            x = x2_[par]
            cb = cbc2_[par]
            for m, bk in zip(ms, banks):
                b.op("vector", lambda E, m=m, bk=bk, cb=cb: E.scalar_tensor_tensor(
                    tmp[:, 0:T], cb[:, T:2 * T], csm[:, m:m + 1], pb[bk][:, 0:T],
                    ALU.mult, ALU.add),
                    (f"cbc{par}", "csm", f"b{bk}"), ("tmp0",), tag=tag + "ep1")
                b.op("vector", lambda E, cb=cb: E.tensor_mul(
                    tmp[:, 0:T], tmp[:, 0:T], cb[:, 0:T]),
                    ("tmp0", f"cbc{par}"), ("tmp0",), tag=tag + "ep2")
                b.op("vector", lambda E, m=m, x=x: E.scalar_tensor_tensor(
                    x[:, m, :], tmp[:, 0:T], halfb[:, m:m + 1], x[:, m, :],
                    ALU.add, ALU.add),
                    ("tmp0", "halfb", f"x{par}_{m}"), (f"x{par}_{m}",), tag=tag + "ep3")

        def tile_front(i, seq, t):
            par = i % 2
            x = x2_[par]
            cb = cbc2_[par]
            tag = f"s{seq}t{t}_"
            ln_stats(i, 7, tag + "ln2_", "apply")
            if t == 0:
                b.op("gpsimd", lambda E: E.memset(halo[:], 0.0),
                     (), ("halo",), tag + "hz")
            b.op("vector", lambda E: E.tensor_copy(xln[:, :, 0:3], halo[:]),
                 ("halo",) + tuple(f"xl{c}" for c in range(NCHK)),
                 ("halo_r",), tag + "hr")
            for c in range(NCHK):
                b.op("vector", lambda E, c=c, x=x, cb=cb: E.tensor_sub(
                    xln[:, c, 3:3 + T], x[:, c, :], cb[:, T:2 * T]),
                    (f"x{par}_{c}", f"cbc{par}", "halo_r"), (f"xl{c}",), tag + "ap1")
                b.op("vector", lambda E, c=c, cb=cb: E.tensor_mul(
                    xln[:, c, 3:3 + T], xln[:, c, 3:3 + T], cb[:, 0:T]),
                    (f"xl{c}", f"cbc{par}"), (f"xl{c}",), tag + "ap2")
                b.op("vector", lambda E, c=c: E.tensor_scalar(
                    h[:, c, :], xln[:, c, 0:T], cw[:, c, 0:1], None, ALU.mult),
                    (f"xl{c}", "halo_r", "cw"), (f"h{c}",), tag + "cv0")
                for ii in range(1, 4):
                    b.op("vector", lambda E, c=c, ii=ii: E.scalar_tensor_tensor(
                        h[:, c, :], xln[:, c, ii:ii + T], cw[:, c, ii:ii + 1],
                        h[:, c, :], ALU.mult, ALU.add),
                        (f"xl{c}", "halo_r", "cw", f"h{c}"), (f"h{c}",), tag + f"cv{ii}")
                b.op("scalar", lambda E, c=c: E.activation(
                    h[:, c, :], h[:, c, :], ACT.Silu),
                    (f"h{c}",), (f"h{c}",), tag + "silu")
                for (bk, w_, wr, n0, n1) in ((0, wq, "wq", 0, 512),
                                             (1, wk, "wk", 0, 512),
                                             (2, wv, "wv", 0, 512),
                                             (3, wv, "wv", 512, 1024),
                                             (4, wb, "wb", 0, H)):
                    b.op("tensor", lambda E, c=c, bk=bk, w_=w_, n0=n0, n1=n1,
                         s=(c == 0), p=(c == NCHK - 1): E.matmul(
                        pb[bk][:, 0:n1 - n0], h[:, c, :], w_[:, c, n0:n1],
                        start=s, stop=p),
                        (f"h{c}", wr), (f"b{bk}",), tag + f"pj{bk}")
            b.op("vector", lambda E: E.tensor_copy(halo[:], xln[:, :, T:T + 3]),
                 tuple(f"xl{c}" for c in range(NCHK)), ("halo",), tag + "hs")

            def l2norm_block(nm, bank, pk, rr, rres, dstfm, dstfm_res):
                b.op("vector", lambda E, pk=pk: E.tensor_copy(big[:, 0:512], pk[:, 0:512]),
                     (bank,), ("big",), tag + nm + "ev")
                b.op("vector", lambda E: E.tensor_mul(
                    big[:, 512:1024], big[:, 0:512], big[:, 0:512]),
                    ("big",), ("big",), tag + nm + "sq")
                b.op("vector", lambda E, rr=rr: E.reduce_sum(
                    rr[:], big[:, 512:1024].rearrange("p (h d) -> p h d", h=H),
                    axis=AX.X),
                    ("big",), (rres,), tag + nm + "ss")
                b.op("vector", lambda E, rr=rr: E.tensor_scalar(
                    rr[:], rr[:], 1e-24, None, ALU.max),
                    (rres,), (rres,), tag + nm + "mx")
                b.op("scalar", lambda E, rr=rr: E.activation(rr[:], rr[:], ACT.Sqrt),
                     (rres,), (rres,), tag + nm + "sr")
                b.op("vector", lambda E, rr=rr: E.reciprocal(rr[:], rr[:]),
                     (rres,), (rres,), tag + nm + "rc")
                b.op("vector", lambda E, rr=rr: E.tensor_mul(
                    big[:, 512:1024].rearrange("p (h d) -> p h d", h=H),
                    big[:, 0:512].rearrange("p (h d) -> p h d", h=H),
                    rr[:].rearrange("p (h o) -> p h o", o=1).broadcast_to([128, H, 128])),
                    ("big", rres), ("big",), tag + nm + "sc")
                if nm == "k":
                    b.op("vector", lambda E: E.tensor_copy(kt[:, :], big[:, 512:1024]),
                         ("big",), ("kt",), tag + "ktc")
                for hh in range(4):
                    b.op("tensor", lambda E, hh=hh: E.transpose(
                        pb[5][:, hh * 128:(hh + 1) * 128],
                        big[:, 512 + hh * 128:512 + (hh + 1) * 128], ident[:]),
                        ("big", "ident"), ("b5",), tag + nm + "tr")
                b.op("vector", lambda E, dstfm=dstfm: E.tensor_copy(
                    dstfm, pb[5][:, 0:512].rearrange("p (h t) -> p h t", h=H)),
                    ("b5",), (dstfm_res,), tag + nm + "tev")

            l2norm_block("q", "b0", pb[0], rq, "rq", qf[:, :, :], "qf")
            l2norm_block("k", "b1", pb[1], rk, "rk", kf[:, :, :], "kf")
            b.op("scalar", lambda E: E.activation(bt[:], pb[4][:, 0:H], ACT.Tanh,
                                                  scale=0.5),
                 ("b4",), ("bt",), tag + "bsig")
            b.op("vector", lambda E: E.tensor_scalar(
                bt[:], bt[:], 0.5, 0.5, ALU.mult, ALU.add),
                ("bt",), ("bt",), tag + "bsig2")
            b.op("vector", lambda E: E.tensor_copy(vt[:, 0:512], pb[2][:, 0:512]),
                 ("b2",), ("vt",), tag + "vev0")
            b.op("scalar", lambda E: E.activation(vt[:, 512:1024], pb[3][:, 0:512], ACT.Copy),
                 ("b3",), ("vt",), tag + "vev1")
            for hh in range(H):
                b.op("vector", lambda E, hh=hh: E.tensor_scalar(
                    vt[:, hh * 256:(hh + 1) * 256], vt[:, hh * 256:(hh + 1) * 256],
                    bt[:, hh:hh + 1], None, ALU.mult),
                    ("vt", "bt"), ("vt",), tag + "vbf")
            for hh in range(H):
                b.op("vector", lambda E, hh=hh: E.tensor_scalar(
                    big[:, 512 + hh * 128:512 + (hh + 1) * 128],
                    kt[:, hh * 128:(hh + 1) * 128], bt[:, hh:hh + 1],
                    None, ALU.mult),
                    ("kt", "bt"), ("big",), tag + "kbs")
            for hh in range(H):
                b.op("tensor", lambda E, hh=hh: E.transpose(
                    pb[5][:, hh * 128:(hh + 1) * 128],
                    big[:, 512 + hh * 128:512 + (hh + 1) * 128], ident[:]),
                    ("big", "ident"), ("b5",), tag + "kbt")
            b.op("vector", lambda E: E.tensor_copy(big[:, 0:512], pb[5][:, 0:512]),
                 ("b5",), ("big",), tag + "kbe")

            def inversion_prep(hp, bka, bkb, bkx, yoff, ybuf, yres, xbuf, xres):
                h0, h1 = 2 * hp, 2 * hp + 1
                htag = tag + f"ip{hp}_"
                b.op("tensor", lambda E, h0=h0, bkx=bkx: E.matmul(
                    pb[bkx][0:64, 0:64], big[:, h0 * 128:h0 * 128 + 64],
                    kf[:, h0, 0:64],
                    start=True, stop=False), ("kf", "big"), (f"b{bkx}",), htag + "g0")
                b.op("tensor", lambda E, h1=h1, bkx=bkx: E.matmul(
                    pb[bkx][0:64, 128:192], big[:, h1 * 128:h1 * 128 + 64],
                    kf[:, h1, 0:64],
                    start=False, stop=True), ("kf", "big"), (f"b{bkx}",), htag + "g2")
                b.op("tensor", lambda E, h0=h0, bkx=bkx: E.matmul(
                    pb[bkx][64:128, 64:128], big[:, h0 * 128 + 64:h0 * 128 + 128],
                    kf[:, h0, 64:128],
                    start=True, stop=False), ("kf", "big"), (f"b{bkx}",), htag + "g1")
                b.op("tensor", lambda E, h1=h1, bkx=bkx: E.matmul(
                    pb[bkx][64:128, 192:256], big[:, h1 * 128 + 64:h1 * 128 + 128],
                    kf[:, h1, 64:128],
                    start=False, stop=True), ("kf", "big"), (f"b{bkx}",), htag + "g3")
                for (r0, c0) in ((0, 64), (64, 0), (0, 192), (64, 128)):
                    b.op("vector", lambda E, r0=r0, c0=c0, bkx=bkx: E.memset(
                        pb[bkx][r0:r0 + 64, c0:c0 + 64], 0.0),
                        (f"b{bkx}",), (f"b{bkx}",), htag + "gz")
                for yo in (0, 256):
                    b.op("vector", lambda E, yo=yo, yoff=yoff, bkx=bkx, ybuf=ybuf,
                         go=(0 if yo == 0 else 128):
                         E.tensor_mul(ybuf[:, yoff + yo:yoff + yo + 128],
                                      pb[bkx][:, go:go + 128], msl[:]),
                         (f"b{bkx}", "msl"), (yres,), htag + "y0m")
                b.op("tensor", lambda E, yoff=yoff, bkx=bkx, ybuf=ybuf: E.transpose(
                    pb[bkx][:, 0:128], ybuf[:, yoff:yoff + 128], ident[:]),
                    (yres, "ident"), (f"b{bkx}",), htag + "z0t0")
                b.op("tensor", lambda E, yoff=yoff, bkx=bkx, ybuf=ybuf: E.transpose(
                    pb[bkx][:, 128:256], ybuf[:, yoff + 256:yoff + 384], ident[:]),
                    (yres, "ident"), (f"b{bkx}",), htag + "z0t1")
                b.op("vector", lambda E, yoff=yoff, bkx=bkx, ybuf=ybuf: E.tensor_copy(
                    ybuf[:, yoff + 128:yoff + 256], pb[bkx][:, 0:128]),
                    (f"b{bkx}",), (yres,), htag + "z0e0")
                b.op("scalar", lambda E, yoff=yoff, bkx=bkx, ybuf=ybuf: E.activation(
                    ybuf[:, yoff + 384:yoff + 512], pb[bkx][:, 128:256], ACT.Copy),
                    (f"b{bkx}",), (yres,), htag + "z0e1")
                b.op("vector", lambda E, yoff=yoff, xo=0, ybuf=ybuf, xbuf=xbuf: E.tensor_add(
                    xbuf[:, xo:xo + 128], ybuf[:, yoff + 128:yoff + 256], ident[:]),
                    (yres, "ident"), (xres,), htag + "x0a")
                b.op("vector", lambda E, yoff=yoff, xo=0, ybuf=ybuf, xbuf=xbuf: E.tensor_add(
                    xbuf[:, xo + 128:xo + 256], ybuf[:, yoff + 384:yoff + 512], ident[:]),
                    (yres, "ident"), (xres,), htag + "x0b")

            def inversion_level(hp, bka, bkb, bkx, yoff, lev, ybuf, yres, xbuf, xres):
                htag = tag + f"ip{hp}_"
                last = lev == 5
                xo = 0
                b.op("tensor", lambda E, last=last, yoff=yoff, bka=bka, ybuf=ybuf: E.matmul(
                    pb[bka][:, 0:128], ybuf[:, yoff + 128:yoff + 256],
                    ybuf[:, yoff:yoff + 128],
                    start=True, stop=last), (yres,), (f"b{bka}",), htag + f"l{lev}ya")
                if not last:
                    b.op("tensor", lambda E, yoff=yoff, bka=bka, ybuf=ybuf: E.matmul(
                        pb[bka][:, 128:256], ybuf[:, yoff:yoff + 128],
                        ybuf[:, yoff + 128:yoff + 256],
                        start=False, stop=True), (yres,), (f"b{bka}",), htag + f"l{lev}za")
                b.op("tensor", lambda E, last=last, yoff=yoff, bkb=bkb, ybuf=ybuf: E.matmul(
                    pb[bkb][:, 0:128], ybuf[:, yoff + 384:yoff + 512],
                    ybuf[:, yoff + 256:yoff + 384],
                    start=True, stop=last), (yres,), (f"b{bkb}",), htag + f"l{lev}yb")
                if not last:
                    b.op("tensor", lambda E, yoff=yoff, bkb=bkb, ybuf=ybuf: E.matmul(
                        pb[bkb][:, 128:256], ybuf[:, yoff + 256:yoff + 384],
                        ybuf[:, yoff + 384:yoff + 512],
                        start=False, stop=True), (yres,), (f"b{bkb}",), htag + f"l{lev}zb")
                n = 128 if last else 256
                b.op("vector", lambda E, n=n, yoff=yoff, bka=bka, ybuf=ybuf: E.tensor_copy(
                    ybuf[:, yoff:yoff + n], pb[bka][:, 0:n]),
                    (f"b{bka}",), (yres,), htag + f"l{lev}ea")
                b.op("scalar", lambda E, n=n, yoff=yoff, bkb=bkb, ybuf=ybuf: E.activation(
                    ybuf[:, yoff + 256:yoff + 256 + n], pb[bkb][:, 0:n], ACT.Copy),
                    (f"b{bkb}",), (yres,), htag + f"l{lev}eb")
                b.op("tensor", lambda E, yoff=yoff, xo=xo, bkx=bkx, ybuf=ybuf, xbuf=xbuf: E.matmul(
                    pb[bkx][:, 0:128], ybuf[:, yoff:yoff + 128], xbuf[:, xo:xo + 128],
                    start=True, stop=False), (yres, xres), (f"b{bkx}",), htag + f"l{lev}xa")
                b.op("tensor", lambda E, yoff=yoff, xo=xo, bkx=bkx, ybuf=ybuf, xbuf=xbuf: E.matmul(
                    pb[bkx][:, 128:256], ybuf[:, yoff + 256:yoff + 384],
                    xbuf[:, xo + 128:xo + 256],
                    start=False, stop=True), (yres, xres), (f"b{bkx}",), htag + f"l{lev}xb")
                if last:
                    b.op("vector", lambda E, hp=hp, xo=xo, bkx=bkx, xbuf=xbuf: E.tensor_add(
                        tinv[:, 2 * hp:2 * hp + 2, :].rearrange("p a b -> p (a b)"),
                        xbuf[:, xo:xo + 256], pb[bkx][:, 0:256]),
                        (xres, f"b{bkx}"), ("tinv",), htag + "tie")
                else:
                    b.op("vector", lambda E, xo=xo, bkx=bkx, xbuf=xbuf: E.tensor_add(
                        xbuf[:, xo:xo + 256], xbuf[:, xo:xo + 256], pb[bkx][:, 0:256]),
                        (xres, f"b{bkx}"), (xres,), htag + f"l{lev}xe")

            inversion_prep(0, 5, 6, 4, 0, yz, "yz", xv, "xv")
            inversion_prep(1, 2, 3, 7, 0, uu, "uu", mq, "mq")
            for lev in range(1, 6):
                inversion_level(0, 5, 6, 4, 0, lev, yz, "yz", xv, "xv")
                inversion_level(1, 2, 3, 7, 0, lev, uu, "uu", mq, "mq")

        def chain(i, seq, t, c):
            p0 = c * 64
            ctag = f"s{seq}t{t}_c{c}_"
            for hh in range(H):
                b.op("tensor", lambda E, hh=hh, c=c, p0=p0, s=(hh == 0), p=(hh == 3):
                     E.matmul(pb[4][p0:p0 + 64, hh * 64:(hh + 1) * 64],
                              kf[:, hh, c * 64:c * 64 + 64],
                              qf[:, hh, c * 64:c * 64 + 64],
                              start=s, stop=p),
                     ("kf", "qf"), ("b4",), ctag + "mq")
            b.op("vector", lambda E, p0=p0: E.tensor_mul(
                mq[p0:p0 + 64, :].rearrange("p (h j) -> p h j", h=H),
                pb[4][p0:p0 + 64, 0:256].rearrange("p (h j) -> p h j", h=H),
                mui[p0:p0 + 64, :].rearrange("p (o j) -> p o j", o=1)
                .broadcast_to([64, H, 64])),
                ("b4", "mui"), ("mq",), ctag + "mqe")
            for hh in range(H):
                bk = 2 + hh // 2
                co = (hh % 2) * 256
                b.op("tensor", lambda E, hh=hh, c=c, p0=p0, bk=bk, co=co,
                     s=(hh % 2 == 0), p=(hh % 2 == 1): E.matmul(
                    pb[bk][p0:p0 + 64, co:co + 256],
                    big[:, hh * 128 + p0:hh * 128 + p0 + 64], S[:, hh, :],
                    start=s, stop=p),
                    ("big", f"S{hh}"), (f"b{bk}",), ctag + "w1")
            for hh in range(H):
                bk = 5 + hh // 2
                co = (hh % 2) * 256
                b.op("tensor", lambda E, hh=hh, c=c, p0=p0, bk=bk, co=co,
                     s=(hh % 2 == 0): E.matmul(
                    pb[bk][p0:p0 + 64, co:co + 256],
                    qf[:, hh, c * 64:c * 64 + 64], S[:, hh, :],
                    start=s, stop=False),
                    ("qf", f"S{hh}"), (f"b{bk}",), ctag + "o1")
            for bk in (2, 3):
                hh0 = (bk - 2) * 2
                b.op("vector", lambda E, bk=bk, hh0=hh0, p0=p0: E.tensor_sub(
                    wp[p0:p0 + 64, hh0 * 256:(hh0 + 2) * 256],
                    vt[p0:p0 + 64, hh0 * 256:(hh0 + 2) * 256],
                    pb[bk][p0:p0 + 64, 0:512]),
                    ("vt", f"b{bk}"), ("wp",), ctag + "wpa")
            for hh in range(H):
                bk = 2 + hh // 2
                co = (hh % 2) * 256
                b.op("tensor", lambda E, hh=hh, p0=p0, bk=bk, co=co: E.matmul(
                    pb[bk][p0:p0 + 64, co:co + 256],
                    tinv[p0:p0 + 64, hh, p0:p0 + 64],
                    wp[p0:p0 + 64, hh * 256:(hh + 1) * 256],
                    start=(hh % 2 == 0), stop=(hh % 2 == 1)),
                    ("tinv", "wp"), (f"b{bk}",), ctag + "u")
            b.op("vector", lambda E, p0=p0: E.tensor_copy(
                uu[p0:p0 + 64, 0:512], pb[2][p0:p0 + 64, 0:512]),
                ("b2",), ("uu",), ctag + "uea")
            b.op("scalar", lambda E, p0=p0: E.activation(
                uu[p0:p0 + 64, 512:1024], pb[3][p0:p0 + 64, 0:512], ACT.Copy),
                ("b3",), ("uu",), ctag + "ueb")
            for hh in range(H):
                bk = 5 + hh // 2
                co = (hh % 2) * 256
                b.op("tensor", lambda E, hh=hh, p0=p0, bk=bk, co=co,
                     p=(hh % 2 == 1): E.matmul(
                    pb[bk][p0:p0 + 64, co:co + 256],
                    mq[p0:p0 + 64, hh * 64:(hh + 1) * 64],
                    uu[p0:p0 + 64, hh * 256:(hh + 1) * 256],
                    start=False, stop=p),
                    ("mq", "uu"), (f"b{bk}",), ctag + "o2")
            b.op("vector", lambda E, p0=p0: E.tensor_copy(
                ot[p0:p0 + 64, 0:512], pb[5][p0:p0 + 64, 0:512]),
                ("b5",), ("ot",), ctag + "oea")
            b.op("scalar", lambda E, p0=p0: E.activation(
                ot[p0:p0 + 64, 512:1024], pb[6][p0:p0 + 64, 0:512], ACT.Copy),
                ("b6",), ("ot",), ctag + "oeb")
            for hh in range(H):
                bk = 2 + hh // 2
                co = (hh % 2) * 256
                b.op("tensor", lambda E, hh=hh, p0=p0, bk=bk, co=co: E.matmul(
                    pb[bk][:, co:co + 256],
                    kt[p0:p0 + 64, hh * 128:(hh + 1) * 128],
                    uu[p0:p0 + 64, hh * 256:(hh + 1) * 256],
                    start=(hh % 2 == 0), stop=(hh % 2 == 1)),
                    ("kt", "uu"), (f"b{bk}",), ctag + "ds")
            b.op("vector", lambda E: E.tensor_add(
                S[:, 0:2, :].rearrange("p a b -> p (a b)"),
                S[:, 0:2, :].rearrange("p a b -> p (a b)"), pb[2][:, 0:512]),
                ("S0", "S1", "b2"), ("S0", "S1"), ctag + "sua")
            b.op("vector", lambda E: E.tensor_add(
                S[:, 2:4, :].rearrange("p a b -> p (a b)"),
                S[:, 2:4, :].rearrange("p a b -> p (a b)"), pb[3][:, 0:512]),
                ("S2", "S3", "b3"), ("S2", "S3"), ctag + "sub")

        def tile_back(i, seq, t, nxt):
            par = i % 2
            x = x2_[par]
            tag = f"s{seq}t{t}_"
            gtag = tag + "g_"
            chain(i, seq, t, 0)
            for c in range(NCHK):
                for half in range(2):
                    b.op("tensor", lambda E, c=c, half=half,
                         s=(c == 0), p=(c == NCHK - 1): E.matmul(
                        pb[half][:, 0:512], h[:, c, :],
                        wg[:, c, half * 512:(half + 1) * 512],
                        start=s, stop=p),
                        (f"h{c}", "wg"), (f"b{half}",), tag + "gmm")
            chain(i, seq, t, 1)
            for half in range(2):
                b.op("scalar", lambda E, half=half: E.activation(
                    big[:, half * 512:(half + 1) * 512], pb[half][:, 0:512], ACT.Silu),
                    (f"b{half}",), ("big",), gtag + "gsil")
            for half in range(2):
                bk = 6 + half
                b.op("scalar", lambda E, half=half, bk=bk: E.activation(
                    pb[bk][:, 0:512], ot[:, half * 512:(half + 1) * 512], ACT.Square),
                    ("ot",), (f"b{bk}",), gtag + "osq")
                b.op("vector", lambda E, half=half, bk=bk: E.reduce_sum(
                    rq[:, half * 2:half * 2 + 2],
                    pb[bk][:, 0:512].rearrange("p (h d) -> p h d", h=2), axis=AX.X),
                    (f"b{bk}",), ("rq",), gtag + "ored")
            b.op("vector", lambda E: E.tensor_scalar(
                rq[:], rq[:], 1.0 / DVH, NORM_EPS, ALU.mult, ALU.add),
                ("rq",), ("rq",), gtag + "eps")
            b.op("scalar", lambda E: E.activation(rq[:], rq[:], ACT.Sqrt),
                 ("rq",), ("rq",), gtag + "sqrt")
            b.op("vector", lambda E: E.reciprocal(rq[:], rq[:]),
                 ("rq",), ("rq",), gtag + "rc")
            if nxt is not None:
                ni, nseq, ntt = nxt
                ntag = f"s{nseq}t{ntt}_"
                nsrc = x_h[nseq].rearrange("(c p) t -> p c t", p=128)[:, :, ntt * T:(ntt + 1) * T]
                dma("sync", x2_[ni % 2][:], nsrc,
                    (), tuple(f"x{ni % 2}_{c}" for c in range(NCHK)), ntag + "ldx")
                ffn_negs(ni, ntag + "ffn1_")
                ffn_wave(ni, (0, 1, 2, 3), (0, 1, 2, 3), ntag + "ffn1_a")
            b.op("vector", lambda E: E.tensor_mul(
                vt[:].rearrange("p (h d) -> p h d", h=H),
                ot[:].rearrange("p (h d) -> p h d", h=H),
                rq[:].rearrange("p (h o) -> p h o", o=1).broadcast_to([128, H, DVH])),
                ("ot", "rq"), ("vt",), gtag + "rms")
            b.op("vector", lambda E: E.tensor_mul(ot[:], vt[:], big[:]),
                 ("vt", "big"), ("ot",), gtag + "gate")
            b.op("vector", lambda E: E.tensor_mul(
                vt[:].rearrange("p (h d) -> p h d", h=H),
                ot[:].rearrange("p (h d) -> p h d", h=H),
                nwb[:].rearrange("p (o d) -> p o d", o=1).broadcast_to([128, H, DVH])),
                ("ot", "nwb"), ("vt",), gtag + "nw")
            for j in range(8):
                bk = 6 if j < 4 else 7
                b.op("tensor", lambda E, j=j, bk=bk: E.transpose(
                    pb[bk][:, (j % 4) * 128:(j % 4 + 1) * 128],
                    vt[:, j * 128:(j + 1) * 128], ident[:]),
                    ("vt", "ident"), (f"b{bk}",), gtag + "otr")
            b.op("vector", lambda E: E.tensor_copy(
                wp[:, 0:512], pb[6][:, 0:512]),
                ("b6",), ("wp",), gtag + "oev0")
            b.op("scalar", lambda E: E.activation(
                wp[:, 512:1024], pb[7][:, 0:512], ACT.Copy),
                ("b7",), ("wp",), gtag + "oev1")
            for m in range(8):
                bk = 4 + m % 2
                for k in range(8):
                    b.op("tensor", lambda E, k=k, m=m, bk=bk, s=(k == 0), p=(k == 7):
                         E.matmul(pb[bk][:, 0:T], wo[:, k, m * 128:(m + 1) * 128],
                                  wp[:, k * 128:(k + 1) * 128], start=s, stop=p),
                         ("wo", "wp"), (f"b{bk}",), gtag + "womm")
                b.op("vector", lambda E, m=m, bk=bk, x=x: E.tensor_add(
                    x[:, m, :], x[:, m, :], pb[bk][:, 0:T]),
                    (f"x{par}_{m}", f"b{bk}"), (f"x{par}_{m}",), gtag + "woadd")
            if nxt is not None:
                ln_stats_mms(ni, 6, ntag + "ln1_")
                ln_stats_smalls(ni, 6, ntag + "ln1_")
                ln_stats_mms(i, 7, tag + "ln3_")
                ln_stats_fin(ni, 6, ntag + "ln1_", "fold")
                ffn_wave(ni, (4, 5, 6), (4, 5, 6), ntag + "ffn1_b")
                ln_stats_smalls(i, 7, tag + "ln3_")
                ln_stats_fin(i, 7, tag + "ln3_", "fold")
                ffn_wave(ni, (7,), (7,), ntag + "ffn1_c")
                ffn_epi(ni, (0, 1, 2, 3), (0, 1, 2, 3), ntag + "ffn1_a")
                ffn_epi(ni, (4, 5, 6), (4, 5, 6), ntag + "ffn1_b")
                ffn_epi(ni, (7,), (7,), ntag + "ffn1_c")
            else:
                ln_stats(i, 7, tag + "ln3_", "fold")
            ffn_negs(i, tag + "ffn2_")
            ffn_wave(i, (0, 1, 2, 3), (0, 1, 2, 3), tag + "ffn2_a")
            ffn_wave(i, (4, 5, 6, 7), (4, 5, 6, 7), tag + "ffn2_b")
            ffn_epi(i, (0, 1, 2, 3), (0, 1, 2, 3), tag + "ffn2_a")
            ffn_epi(i, (4, 5, 6, 7), (4, 5, 6, 7), tag + "ffn2_b")
            dstv = y_h[seq].rearrange("(c p) t -> p c t", p=128)[:, :, t * T:(t + 1) * T]
            b.op("scalar", lambda E, d=dstv, x=x: E.dma_start(out=d, in_=x[:]),
                 tuple(f"x{par}_{c}" for c in range(NCHK)), (), dma=True,
                 tag=tag + "st", force=True)

        tiles = [(i, i // NTILE, i % NTILE) for i in range(BPC * NTILE)]
        i0, s0, t0 = tiles[0]
        src0 = x_h[s0].rearrange("(c p) t -> p c t", p=128)[:, :, t0 * T:(t0 + 1) * T]
        dma("sync", x2_[0][:], src0, (), tuple(f"x0_{c}" for c in range(NCHK)), "s0t0_ldx")
        ln_stats(0, 7, "s0t0_ln1_", "fold")
        ffn_negs(0, "s0t0_ffn1_")
        ffn_wave(0, (0, 1, 2, 3), (0, 1, 2, 3), "s0t0_ffn1_a")
        ffn_wave(0, (4, 5, 6, 7), (4, 5, 6, 7), "s0t0_ffn1_b")
        ffn_epi(0, (0, 1, 2, 3), (0, 1, 2, 3), "s0t0_ffn1_a")
        ffn_epi(0, (4, 5, 6, 7), (4, 5, 6, 7), "s0t0_ffn1_b")
        for hh in range(H):
            b.op("gpsimd", lambda E, hh=hh: E.memset(S[:, hh, :], 0.0),
                 (), (f"S{hh}",), "s0_szero")
        for idx, (i, seq, t) in enumerate(tiles):
            if t == 0 and seq > 0:
                for hh in range(H):
                    b.op("gpsimd", lambda E, hh=hh: E.memset(S[:, hh, :], 0.0),
                         tuple(), (f"S{hh}",), f"s{seq}_szero")
            tile_front(i, seq, t)
            nxt = None
            if idx + 1 < len(tiles):
                nxt = tiles[idx + 1]
            tile_back(i, seq, t, nxt)

        b.finalize()
        out_cum = max(o.cum for o in b.plan["scalar"] if o.dma)

        with nc.Block() as block:
            @block.sync
            def _(eng):
                b.replay("sync", eng, sems)

            @block.tensor
            def _(eng):
                b.replay("tensor", eng, sems)

            @block.vector
            def _(eng):
                b.replay("vector", eng, sems)

            @block.scalar
            def _(eng):
                b.replay("scalar", eng, sems)
                eng.wait_ge(sems["scalar"][1], out_cum)

            @block.gpsimd
            def _(eng):
                b.replay("gpsimd", eng, sems)

    return nc


def _host_stage(inputs):
    xs = np.asarray(inputs["L0_attn_hidden_states"], np.float32)
    G1R = np.asarray(inputs["L0_ffn_G1_R"], np.float32)
    G2R = np.asarray(inputs["L0_ffn_G2_R"], np.float32)
    G1I = np.asarray(inputs["L0_ffn_G1_I"], np.float32)
    G2I = np.asarray(inputs["L0_ffn_G2_I"], np.float32)
    acr = np.zeros((128, 2, 4, 4, 32), np.float32)
    bcr = np.zeros((128, 2, 4, 16), np.float32)
    p_idx = np.arange(128)
    for ri, (G1, G2) in enumerate(((G1R, G2R), (G1I, G2I))):
        for c in range(4):
            for kc in range(4):
                d_idx = (kc * 128 + p_idx) // 16
                acr[:, ri, c, kc, :] = G1[0, :, c, :].T[d_idx, :]
            g_idx = p_idx % 16
            bcr[:, ri, c, :] = G2[c, :, 0, :].T[g_idx, :]
    biasri = np.concatenate([np.asarray(inputs["L0_ffn_bias_real"], np.float32),
                             np.asarray(inputs["L0_ffn_bias_imag"], np.float32)])
    biasri = np.ascontiguousarray(biasri.reshape(8, 128).T)
    nw = np.ascontiguousarray(np.asarray(inputs["L0_attn_norm_w"],
                                         np.float32).reshape(1, 256))
    shared = {
        "wq": np.ascontiguousarray(inputs["L0_attn_Wq"], np.float32),
        "wk": np.ascontiguousarray(inputs["L0_attn_Wk"], np.float32),
        "wv": np.ascontiguousarray(inputs["L0_attn_Wv"], np.float32),
        "wg": np.ascontiguousarray(inputs["L0_attn_Wg"], np.float32),
        "wo": np.ascontiguousarray(inputs["L0_attn_Wo"], np.float32),
        "wb": np.ascontiguousarray(inputs["L0_attn_Wb"], np.float32),
        "cw": np.ascontiguousarray(inputs["L0_attn_conv_w"], np.float32),
        "acr": acr, "bcr": bcr, "biasri": biasri, "normw": nw,
    }
    in_maps = []
    for core in range(NCORE):
        xc = np.ascontiguousarray(
            xs[core * BPC:(core + 1) * BPC].transpose(0, 2, 1))
        m = dict(shared)
        m["x_in"] = xc
        in_maps.append(m)
    return in_maps


def kernel(**inputs):
    in_maps = _host_stage(inputs)
    nc = build_program()
    res = run_bass_kernel_spmd(nc, in_maps, core_ids=list(range(NCORE)))
    outs = []
    for core in range(NCORE):
        yc = np.asarray(res.results[core]["y_out"])
        outs.append(yc.transpose(0, 2, 1))
    return np.ascontiguousarray(np.concatenate(outs, axis=0))
